# version 03 (final)
# baseline (speedup 1.0000x reference)
from contextlib import ExitStack
import numpy as np
import ml_dtypes
import concourse.bass as bass
import concourse.mybir as mybir
from concourse.bass_utils import run_bass_kernel_spmd

F32 = mybir.dt.float32
BF16 = mybir.dt.bfloat16
AX = mybir.AxisListType.X
AF = mybir.ActivationFunctionType
OP = mybir.AluOpType

T, D = 2048, 1024
TC, DC, NB = T // 128, D // 128, T // 512
H, DH, J, DM, DF, E, I = 16, 64, 64, 256, 4096, 16, 2048

DN = 64 ** (-0.25)
DIAG_C = 0.5 * DN * DN

DEBUG_TAPS = ()
STOP_AFTER = None
ENGINES = ("sync", "tensor", "vector", "scalar", "gpsimd")


class Sem:
    def __init__(self, nc, name):
        self.h = nc.alloc_semaphore(name)
        self.n = 0


class Prog:
    def __init__(self, nc):
        self.nc = nc
        self.q = {e: [] for e in ENGINES}
        self.sems = {}

    def sem(self, name):
        if name not in self.sems:
            self.sems[name] = Sem(self.nc, name)
        return self.sems[name]

    SERIAL = {"vector": "V", "scalar": "S", "gpsimd": "G"}

    def emit(self, eng, fn, wait=(), inc=None, inc_amt=1, hazard=False):
        wl = []
        last = getattr(self, "last", None)
        if last is None:
            last = self.last = {}
        if eng in self.SERIAL:
            pend = last.get(eng)
            if pend is not None:
                wait = list(wait) + [pend]
            if inc is None and fn is not None:
                inc = self.sem(self.SERIAL[eng])
        for tk in wait:
            if tk is None:
                continue
            s, v = tk
            if s in getattr(self, "promote", ()):
                v = s.n
            assert v <= s.n, f"wait on future value {v} > {s.n} ({s.h.name})"
            wl.append((s.h, v))
        tok = None
        if inc is not None:
            inc.n += inc_amt
            tok = (inc, inc.n)
        self.q[eng].append((wl, fn, (inc.h if inc else None), inc_amt))
        if eng in self.SERIAL and fn is not None:
            last[eng] = tok
        return tok

    def finish(self, issuers):
        for eng, sem in issuers:
            self.q[eng].append(([(sem.h, sem.n)], None, None, 0))

    def barrier(self):
        b = self.sem("barrier")
        base = b.n
        for e in ENGINES:
            self.emit(e, None, inc=b)
        for e in ENGINES:
            self.q[e].append(([(b.h, base + len(ENGINES))], None, None, 0))

    def build(self, block):
        def mk(name):
            def body(eng):
                for wl, fn, inch, amt in self.q[name]:
                    for h, v in wl:
                        eng.wait_ge(h, v)
                    inst = fn(eng) if fn is not None else (
                        eng.nop() if inch is not None else None)
                    if inch is not None:
                        inst.then_inc(inch, amt)
            return body
        for reg, name in ((block.sync, "sync"), (block.tensor, "tensor"),
                          (block.vector, "vector"), (block.scalar, "scalar"),
                          (block.gpsimd, "gpsimd")):
            reg(mk(name))


class Rot:

    def __init__(self, items):
        self.items = list(items)
        self.tok = [None] * len(self.items)
        self.i = 0

    def get(self):
        i = self.i % len(self.items)
        self.i += 1
        return i, self.items[i], ([self.tok[i]] if self.tok[i] else [])

    def done(self, i, tok):
        self.tok[i] = tok


def emit_round(p, V, v_ap, out_ap, r, cb):
    p.emit("vector", lambda e: e.tensor_scalar(r, v_ap, 256.0, None, OP.add), inc=V)
    for b in [256.0, 128.0, 64.0, 32.0, 16.0, 8.0, 4.0, 2.0, 1.0]:
        p.emit("vector", lambda e, b=b: e.tensor_scalar(cb, r, b, -b, OP.is_ge, OP.mult), inc=V)
        p.emit("vector", lambda e: e.tensor_tensor(r, r, cb, OP.add), inc=V)
    p.emit("vector", lambda e: e.scalar_tensor_tensor(out_ap, v_ap, 256.0, r, OP.add, OP.subtract), inc=V)
    p.emit("vector", lambda e: e.tensor_scalar(v_ap, r, 0.5, None, OP.is_gt), inc=V)
    p.emit("vector", lambda e: e.tensor_tensor(out_ap, out_ap, v_ap, OP.add), inc=V)
    p.emit("vector", lambda e: e.tensor_scalar(v_ap, r, 0.5, None, OP.is_equal), inc=V)
    p.emit("vector", lambda e: e.tensor_tensor(v_ap, v_ap, cb, OP.mult), inc=V)
    p.emit("vector", lambda e: e.tensor_tensor(out_ap, out_ap, v_ap, OP.subtract), inc=V)
    return p.emit("vector", lambda e: e.tensor_scalar(out_ap, out_ap, -256.0, None, OP.add), inc=V)


def emit_rsqrt(p, u_ap, out_ap, tmp_ap, wait=(), newton=True):
    S, V = p.sem("S"), p.sem("V")
    p.emit("scalar", lambda e, a=tmp_ap, b=u_ap: e.activation(a, b, AF.Ln),
           wait=wait, inc=S)
    t2 = p.emit("scalar", lambda e, a=out_ap, b=tmp_ap: e.activation(
        a, b, AF.Exp, scale=-0.5), inc=S)
    if not newton:
        return t2
    p.emit("vector", lambda e, a=tmp_ap, r=out_ap: e.tensor_tensor(a, r, r, OP.mult),
           wait=[t2], inc=V)
    p.emit("vector", lambda e, a=tmp_ap, u=u_ap: e.tensor_tensor(a, a, u, OP.mult), inc=V)
    p.emit("vector", lambda e, a=tmp_ap: e.tensor_scalar(
        a, a, -0.5, 1.5, OP.mult, OP.add), inc=V)
    return p.emit("vector", lambda e, r=out_ap, a=tmp_ap: e.tensor_tensor(
        r, r, a, OP.mult), inc=V)


def emit_recip(p, x_ap, out_ap, tmp_ap, wait=(), newton=True):
    S, V = p.sem("S"), p.sem("V")
    p.emit("scalar", lambda e, a=tmp_ap, b=x_ap: e.activation(a, b, AF.Ln),
           wait=wait, inc=S)
    t2 = p.emit("scalar", lambda e, a=out_ap, b=tmp_ap: e.activation(
        a, b, AF.Exp, scale=-1.0), inc=S)
    if not newton:
        return t2
    p.emit("vector", lambda e, a=tmp_ap, x=x_ap, r=out_ap: e.tensor_tensor(
        a, x, r, OP.mult), wait=[t2], inc=V)
    p.emit("vector", lambda e, a=tmp_ap: e.tensor_scalar(
        a, a, -1.0, 2.0, OP.mult, OP.add), inc=V)
    return p.emit("vector", lambda e, r=out_ap, a=tmp_ap: e.tensor_tensor(
        r, r, a, OP.mult), inc=V)


def build_nc(taps=()):
    nc = bass.Bass("TRN2", target_bir_lowering=False, detect_race_conditions=False)

    def din(name, shape, dtype=F32):
        return nc.dram_tensor(name, shape, dtype, kind="ExternalInput").ap()

    x_h = din("x", [T, D])
    wq_h, wk_h, wv_h, wo_h = din("wq", [D, D]), din("wk", [D, D]), din("wv", [D, D]), din("wo", [D, D])
    bq_h, bk_h, bv_h, bo_h = din("bq", [D]), din("bk", [D]), din("bv", [D]), din("bo", [D])
    proj_h = din("proj", [H, J, DH])
    wf_h, bf_h = din("wf", [D, D]), din("bf", [D])
    eye_h = din("eye", [128, 128])
    eyebf_h = din("eyebf", [128, 128], BF16)
    onesrow_h = din("onesrow", [1, 128])
    onescol_h = din("onescol", [128, 1])

    wdec_h, wgat_h = din("wdec", [D, D]), din("wgat", [D, D])
    bdecay_h, bgate_h = din("bdec", [D]), din("bgat", [D])
    convw_h = din("convw", [D, 7])
    n1w_h, n2w_h = din("n1w", [D]), din("n2w", [D])
    wk1_h, m_h, wmem_h = din("wk1", [D, DM]), din("mm", [DM, DM]), din("wmem", [DM, D])
    wc_h, bc_h = din("wc", [3 * D, D]), din("bc", [D])
    wup_h, wg_h, wdn_h = din("wup", [D, DF]), din("wg", [D, DF]), din("wdn", [DF, D])
    rw_h, rb_h = din("rw", [D, E]), din("rb", [E])
    w1_h, b1_h = din("w1", [E, D, I]), din("b1", [E, I])
    w2_h, b2_h = din("w2", [E, I, D]), din("b2", [E, D])
    ss_h, msc_h = din("ss", [1, 1]), din("msc", [1, 1])
    xn1_sp = nc.dram_tensor("xn1_sp", [128, DC, T], BF16, kind="Internal").ap()
    state_sp = nc.dram_tensor("state_sp", [128, DC, T], BF16, kind="Internal").ap()
    local_sp = nc.dram_tensor("local_sp", [128, DC, T], BF16, kind="Internal").ap()
    mem_sp = nc.dram_tensor("mem_sp", [128, DC, T], BF16, kind="Internal").ap()
    mbf_h = nc.dram_tensor("mbf", [DM, DM], BF16, kind="Internal").ap()
    out_h = nc.dram_tensor("out", [T, D], F32, kind="ExternalOutput").ap()
    out_r = out_h.rearrange("(c p) d -> p c d", p=128)
    tap_aps = {}

    p = Prog(nc)
    DMA, GD, TT, V, S, ST = (p.sem(n) for n in ("DMA", "GD", "TT", "V", "S", "ST"))
    p.promote = {DMA, GD, ST}

    def store_tap(name, src_ap, shape, dtype, wait):
        if name not in taps:
            return None
        a = nc.dram_tensor("tap_" + name, list(shape), dtype, kind="ExternalOutput").ap()
        tap_aps[name] = a
        return p.emit("sync", lambda e, a=a, s=src_ap: e.dma_start(out=a, in_=s),
                      wait=wait, inc=ST, inc_amt=16)

    with ExitStack() as stack:
        sb = lambda *a: stack.enter_context(nc.sbuf_tensor(*a))
        psb = lambda *a: stack.enter_context(nc.psum_tensor(*a))
        acc = sb("acc", [128, TC, D], F32)
        xn_fm = sb("xn_fm", [128, DC, T], BF16)
        eye = sb("s_eye", [128, 128], F32)
        eyebf = sb("s_eyebf", [128, 128], BF16)
        onesrow = sb("s_onesrow", [1, 128], F32)
        onescol = sb("s_onescol", [128, 1], F32)
        onescol_bf = sb("s_onescol_bf", [128, 1], BF16)
        stats = sb("stats", [128, TC, 8], F32)
        qcol = sb("qcol", [128, 32], F32)
        smalls = sb("smalls", [128, 64], F32)
        rows = sb("rows", [1, 2 * D + 128], F32)
        stW = stack.enter_context(ExitStack())
        sbw = lambda *a: stW.enter_context(nc.sbuf_tensor(*a))
        wA = sbw("wA", [128, DC, D], BF16)
        wB = sbw("wB", [128, DC, D], BF16)
        wqn = sbw("wqn", [128, D], BF16)
        projT = sbw("projT", [128, 8, 128], BF16)
        wsrep = sbw("wsrep", [128, D], F32)
        bfrep = sbw("bfrep", [128, D], F32)
        diag_k = sbw("diag_k", [128, TC, H], F32)
        stabk = sbw("stabk", [128, H], F32)
        kv_bd = sbw("kv_bd", [128, 8, 128], BF16)
        ksum_bd = sbw("ksum_bd", [128, 8, 2], BF16)
        psA = psb("psA", [128, 512], F32)
        psB = psb("psB", [128, 512], F32)
        psC = psb("psC", [128, 512], F32)
        psD = psb("psD", [128, 512], F32)
        block = stack.enter_context(nc.Block())
        pool = Rot([psA, psB, psC, psD])

        p.emit("vector", lambda e: e.memset(stats[:], 0.0), inc=p.sem("V"))
        p.emit("vector", lambda e: e.memset(smalls[:], 0.0), inc=p.sem("V"))
        t_eye = p.emit("sync", lambda e: e.dma_start(out=eye[:], in_=eye_h), inc=DMA, inc_amt=16)
        t_eyebf = p.emit("sync", lambda e: e.dma_start(out=eyebf[:], in_=eyebf_h), inc=DMA, inc_amt=16)
        t_or = p.emit("sync", lambda e: e.dma_start(out=onesrow[:], in_=onesrow_h), inc=DMA, inc_amt=16)
        t_oc = p.emit("sync", lambda e: e.dma_start(out=onescol[:], in_=onescol_h), inc=DMA, inc_amt=16)
        t_x = p.emit("sync", lambda e: e.dma_start(
            out=acc[:], in_=x_h.rearrange("(c p) d -> p c d", p=128)), inc=DMA, inc_amt=16)
        p.emit("scalar", lambda e: e.copy(onescol_bf[:], onescol[:]), wait=[t_oc], inc=S)
        t_bq = p.emit("sync", lambda e: e.dma_start(
            out=qcol[:, 0:8], in_=bq_h.rearrange("(c p) -> p c", p=128), allow_slow_non_contiguous=True), inc=DMA, inc_amt=16)
        t_bk = p.emit("sync", lambda e: e.dma_start(
            out=qcol[:, 8:16], in_=bk_h.rearrange("(c p) -> p c", p=128), allow_slow_non_contiguous=True), inc=DMA, inc_amt=16)
        t_bv = p.emit("sync", lambda e: e.dma_start(
            out=qcol[:, 16:24], in_=bv_h.rearrange("(c p) -> p c", p=128), allow_slow_non_contiguous=True), inc=DMA, inc_amt=16)
        t_bo = p.emit("sync", lambda e: e.dma_start(out=rows[0:1, 0:D], in_=bo_h.unsqueeze(0)), inc=DMA, inc_amt=16)
        t_bvr = p.emit("sync", lambda e: e.dma_start(out=rows[0:1, D:2 * D], in_=bv_h.unsqueeze(0)), inc=DMA, inc_amt=16)

        store_tap("acc_raw", acc[:], [128, TC, D], F32, [t_x])
        with ExitStack() as st0:
            wnat = st0.enter_context(nc.sbuf_tensor("wnat", [128, DC, D], BF16))
            xsc = st0.enter_context(nc.sbuf_tensor("xsc", [128, 2, D], F32))
            sqscr = st0.enter_context(nc.sbuf_tensor("sqscr", [128, D], F32))
            sq_prev = None
            ln_tok = [None] * TC
            for tc in range(TC):
                cc = [stats[:, tc, j:j + 1] for j in range(8)]
                p.emit("vector", lambda e, a=cc[0], s=acc[:, tc, :]: e.reduce_sum(
                    a, s, axis=AX), wait=[t_x], inc=V)
                tq = p.emit("scalar", lambda e, s=acc[:, tc, :]: e.activation(
                    sqscr[:], s, AF.Square), wait=[t_x, sq_prev], inc=S)
                p.emit("vector", lambda e, a=cc[0]: e.tensor_scalar(
                    a, a, 1.0 / D, None, OP.mult), inc=V)
                sq_prev = p.emit("vector", lambda e, a=cc[4]: e.reduce_sum(
                    a, sqscr[:], axis=AX), wait=[tq], inc=V)
                p.emit("vector", lambda e, a=cc[4]: e.tensor_scalar(
                    a, a, 1.0 / D, None, OP.mult), inc=V)
                p.emit("vector", lambda e, a=cc[5], m=cc[0]: e.scalar_tensor_tensor(
                    a, m, -1.0, m, OP.mult, OP.mult), inc=V)
                p.emit("vector", lambda e, a=cc[4], b=cc[5]: e.tensor_tensor(a, a, b, OP.add), inc=V)
                p.emit("vector", lambda e, a=cc[4]: e.tensor_scalar(
                    a, a, 1e-5, None, OP.add), inc=V)
                tu = p.emit("vector", lambda e: e.engine_nop(), inc=V)
                ln_tok[tc] = emit_rsqrt(p, cc[4], cc[1], cc[5], wait=[tu])

            store_tap("stats_early", stats[:], [128, TC, 8], F32, [ln_tok[-1], (p.sem("S"), p.sem("S").n)])
            if STOP_AFTER == "ln":
                p.finish([("sync", DMA), ("sync", ST), ("gpsimd", GD)])
                p.build(block)
                return nc, tap_aps
            xrot = Rot([0, 1])
            t_xn = None
            for tc in range(TC):
                xi, xs_, xw = xrot.get()
                tv = p.emit("vector", lambda e, o=xsc[:, 0, :] if False else None, tc=tc, xs_=xs_:
                            e.tensor_scalar(xsc[:, xs_, :], acc[:, tc, :],
                                            stats[:, tc, 0:1], stats[:, tc, 1:2],
                                            OP.subtract, OP.mult),
                            wait=[ln_tok[tc]] + xw, inc=V)
                te = None
                for dc in range(DC):
                    pi, ps, fw = pool.get()
                    tt = p.emit("tensor", lambda e, dc=dc, ps=ps, xs_=xs_: e.transpose(
                        ps[:, 0:128], xsc[:, xs_, dc * 128:(dc + 1) * 128], eye[:]),
                        wait=[tv, t_eye] + fw, inc=TT)
                    te = p.emit("scalar", lambda e, tc=tc, dc=dc, ps=ps: e.copy(
                        xn_fm[:, dc, tc * 128:(tc + 1) * 128], ps[:, 0:128]),
                        wait=[tt], inc=S)
                    pool.done(pi, te)
                xrot.done(xi, te)
                t_xn = te
            store_tap("xn_fm", xn_fm[:], [128, DC, T], BF16, [t_xn])
            store_tap("stats", stats[:], [128, TC, 8], F32, [(p.sem("V"), p.sem("V").n)])
            if STOP_AFTER == "xnfm":
                p.finish([("sync", DMA), ("sync", ST), ("gpsimd", GD)])
                p.build(block)
                return nc, tap_aps

            wbf = {nm: nc.dram_tensor("wbf_" + nm, [D, D], BF16, kind="Internal").ap()
                   for nm in ("q", "k", "v", "o", "fq")}
            t_cast = {}
            for nm, hh in (("k", wk_h), ("v", wv_h), ("q", wq_h), ("o", wo_h)):
                t_cast[nm] = p.emit("gpsimd", lambda e, hh=hh, nm=nm: e.dma_start(
                    out=wbf[nm], in_=hh), inc=GD, inc_amt=16)

            def load_wT(nm, dst, extra_wait=()):
                tl = None
                for dc in range(DC):
                    tl = p.emit("sync", lambda e, nm=nm, dc=dc, dst=dst: e.dma_start_transpose(
                        dst[:, dc, :], wbf[nm][:, dc * 128:(dc + 1) * 128]),
                        wait=([t_cast.get(nm)] + list(extra_wait)) if dc == 0 else [],
                        inc=DMA, inc_amt=16)
                return tl

            t_wk = load_wT("k", wA)
            t_wv = load_wT("v", wB)

            tpl = p.emit("gpsimd", lambda e: e.dma_start(
                out=wnat[0:64, 0, 0:H * DH].rearrange("p (h d) -> p h d", h=H),
                in_=proj_h.transpose([1, 0, 2])), inc=GD, inc_amt=16)
            tz = p.emit("vector", lambda e: e.memset(projT[:], 0.0), inc=V)
            t_pr = None
            for h in range(H):
                hp, wh = h // 2, h % 2
                pi, ps, fw = pool.get()
                tt = p.emit("tensor", lambda e, h=h, ps=ps: e.transpose(
                    ps[:].bitcast(BF16)[0:64, 0:64],
                    wnat[0:64, 0, h * DH:(h + 1) * DH], eyebf[0:64, 0:64]),
                    wait=[tpl, tz] + fw, inc=TT)
                t_pr = p.emit("scalar", lambda e, hp=hp, wh=wh, ps=ps: e.activation(
                    projT[wh * 64:(wh + 1) * 64, hp, wh * 64:(wh + 1) * 64],
                    ps[:].bitcast(BF16)[0:64, 0:64], AF.Copy, scale=DN), wait=[tt], inc=S)
                pool.done(pi, t_pr)

            if STOP_AFTER == "wprep":
                p.finish([("sync", DMA), ("sync", ST), ("gpsimd", GD)])
                p.build(block)
                return nc, tap_aps

        p.barrier()
        with ExitStack() as stq:
            wf32 = stq.enter_context(nc.sbuf_tensor("wf32", [128, 4, D], F32))
            rsc = stq.enter_context(nc.sbuf_tensor("rsc", [128, 4, 512], F32))
            rows2 = stq.enter_context(nc.sbuf_tensor("rows2", [1, 2 * D], F32))
            t_bf = p.emit("sync", lambda e: e.dma_start(
                out=rows2[0:1, 0:D], in_=bf_h.unsqueeze(0)), inc=DMA, inc_amt=16)
            t_amax = None
            t_wqn = None
            for ph in range(2):
                t_wf = p.emit("sync", lambda e, ph=ph: e.dma_start(
                    out=wf32[:],
                    in_=wf_h[ph * 512:(ph + 1) * 512, :].rearrange("(c p) d -> p c d", p=128)),
                    wait=[t_wqn], inc=DMA, inc_amt=16)
                for oc4 in range(4):
                    oc = ph * 4 + oc4
                    p.emit("vector", lambda e, oc=oc, oc4=oc4: e.reduce_max(
                        smalls[:, oc:oc + 1], wf32[:, oc4, :], axis=AX,
                        apply_absolute_value=True), wait=[t_wf], inc=V)
                    p.emit("vector", lambda e, oc=oc: e.tensor_scalar(
                        smalls[:, oc:oc + 1], smalls[:, oc:oc + 1], 1e-8, None, OP.max), inc=V)
                    tr_ = emit_recip(p, smalls[:, oc:oc + 1], smalls[:, 8 + oc:9 + oc],
                                     smalls[:, 24 + oc:25 + oc],
                                     wait=[(p.sems["V"], p.sems["V"].n)])
                    t_amax = p.emit("vector", lambda e, oc=oc: e.tensor_scalar(
                        smalls[:, 8 + oc:9 + oc], smalls[:, 8 + oc:9 + oc], 127.0,
                        None, OP.mult), wait=[tr_], inc=V)
                    for hf in range(2):
                        p.emit("vector", lambda e, oc=oc, oc4=oc4, hf=hf: e.tensor_scalar(
                            rsc[:, 0, :], wf32[:, oc4, hf * 512:(hf + 1) * 512],
                            smalls[:, 8 + oc:9 + oc], None, OP.mult), inc=V)
                        emit_round(p, V, rsc[:, 0, :], rsc[:, 1, :], rsc[:, 2, :], rsc[:, 3, :])
                        tcl = p.emit("vector", lambda e: e.tensor_scalar(
                            rsc[:, 1, :], rsc[:, 1, :], -128.0, 127.0, OP.max, OP.min), inc=V)
                        t_wqn_s = p.emit("scalar", lambda e, oc=oc, hf=hf: e.copy(
                            wqn[:, hf * 512:(hf + 1) * 512], rsc[:, 1, :]),
                            wait=[tcl], inc=S)
                        t_wqn = t_wqn_s
                    t_wqd = p.emit("gpsimd", lambda e, oc=oc: e.dma_start(
                        out=wbf["fq"][oc * 128:(oc + 1) * 128, :], in_=wqn[:]),
                        wait=[t_wqn], inc=GD, inc_amt=16)
                    t_wqn = t_wqd
            pi, ps, fw = pool.get()
            t1 = p.emit("tensor", lambda e, ps=ps: e.transpose(
                ps[0:8, 0:128], smalls[:, 0:8], eye[:]), wait=[t_amax] + fw, inc=TT)
            t2 = p.emit("vector", lambda e, ps=ps: e.tensor_scalar(
                rsc[0:8, 0, 0:128], ps[0:8, 0:128], 1.0 / 127.0, None, OP.mult),
                wait=[t1, t_wqn], inc=V)
            pool.done(pi, t2)
            t3 = p.emit("sync", lambda e: e.dma_start(
                out=rows2[0:1, D:2 * D], in_=rsc[0:8, 0, 0:128]), wait=[t2],
                inc=DMA, inc_amt=16)
            for half in range(2):
                for dst, off, w0 in ((wsrep, D, t3), (bfrep, 0, t_bf)):
                    pi, ps, fw = pool.get()
                    tt = p.emit("tensor", lambda e, half=half, off=off, ps=ps: e.matmul(
                        ps[:, 0:512], onesrow[:],
                        rows2[0:1, off + half * 512:off + (half + 1) * 512],
                        start=True, stop=True), wait=[w0, t_or] + fw, inc=TT)
                    te = p.emit("vector", lambda e, half=half, ps=ps, dst=dst: e.tensor_copy(
                        dst[:, half * 512:(half + 1) * 512], ps[:, 0:512]),
                        wait=[tt], inc=V)
                    pool.done(pi, te)

            p.barrier()

        if STOP_AFTER == "quant":
            p.finish([("sync", DMA), ("sync", ST), ("gpsimd", GD)])
            p.build(block)
            return nc, tap_aps

        with ExitStack() as st1:
            kq_blk = st1.enter_context(nc.sbuf_tensor("kq_blk", [128, DC, 512], BF16))
            diag_q = st1.enter_context(nc.sbuf_tensor("diag_q", [128, TC, H], F32))
            st1a = st1.enter_context(ExitStack())
            sq_blk = st1a.enter_context(nc.sbuf_tensor("sq_blk", [128, DC, 512], F32))
            runmax = st1a.enter_context(nc.sbuf_tensor("runmax", [128, H], F32))
            esc = st1a.enter_context(nc.sbuf_tensor("esc", [128, 4, 512], F32))
            erot = Rot([0, 1, 2, 3])
            t_rm = p.emit("vector", lambda e: e.memset(runmax[:], -1e30), inc=V)

            def kq_pass(wT, bias_off, tb, dst_diag, do_runmax, t_w, prev_tt):
                ztoks = []
                for oc in range(DC):
                    pi, ps, fw = pool.get()
                    tt = None
                    for dc in range(DC):
                        tt = p.emit("tensor", lambda e, oc=oc, dc=dc, ps=ps, tb=tb, wT=wT: e.matmul(
                            ps[:, 0:512], wT[:, dc, oc * 128:(oc + 1) * 128],
                            xn_fm[:, dc, tb * 512:(tb + 1) * 512],
                            start=(dc == 0), stop=(dc == DC - 1)),
                            wait=([t_xn, t_w, t_bk, t_bq] + fw if dc == 0 else []),
                            inc=(TT if dc == DC - 1 else None))
                    ei, es, ew = erot.get()
                    tz_ = p.emit("vector", lambda e, oc=oc, ps=ps, es=es, bo_=bias_off: e.tensor_scalar(
                        esc[:, es, :], ps[:, 0:512],
                        qcol[:, bo_ + oc:bo_ + oc + 1], None, OP.add),
                        wait=[tt] + ew, inc=V)
                    pool.done(pi, tz_)
                    tb16 = p.emit("scalar", lambda e, oc=oc, es=es: e.copy(
                        kq_blk[:, oc, :], esc[:, es, :]),
                        wait=[tz_, prev_tt], inc=S)
                    tsq = p.emit("vector", lambda e, oc=oc, es=es: e.tensor_tensor(
                        sq_blk[:, oc, :], esc[:, es, :], esc[:, es, :], OP.mult),
                        wait=[prev_tt], inc=V)
                    erot.done(ei, tsq)
                    ztoks.append((tb16, tsq))
                last_tt = None
                for tci in range(4):
                    tc = tb * 4 + tci
                    pi, ps, fw = pool.get()
                    tdg = None
                    for h in range(H):
                        oc, hf = h // 2, h % 2
                        w = (fw + [t_oc, ztoks[oc][1]]) if h == 0 else (
                            [ztoks[oc][1]] if hf == 0 else [])
                        tdg = p.emit("tensor", lambda e, h=h, oc=oc, hf=hf, tci=tci, ps=ps: e.matmul(
                            ps[:, h:h + 1],
                            sq_blk[hf * 64:(hf + 1) * 64, oc, tci * 128:(tci + 1) * 128],
                            onescol[hf * 64:(hf + 1) * 64, 0:1],
                            start=True, stop=True), wait=w,
                            inc=(TT if h == H - 1 else None))
                    tde = p.emit("vector", lambda e, tc=tc, ps=ps, dd=dst_diag: e.tensor_scalar(
                        dd[:, tc, :], ps[:, 0:H], DIAG_C, None, OP.mult),
                        wait=[tdg], inc=V)
                    pool.done(pi, tde)
                    for h2 in range(2):
                        pi2, ps2, fw2 = pool.get()
                        tdd = None
                        for hpi in range(4):
                            hp = h2 * 4 + hpi
                            w = (fw2 + [t_pr, ztoks[hp][0]]) if hpi == 0 else [ztoks[hp][0]]
                            tdd = p.emit("tensor", lambda e, hp=hp, hpi=hpi, tci=tci, ps2=ps2: e.matmul(
                                ps2[:, hpi * 128:(hpi + 1) * 128],
                                kq_blk[:, hp, tci * 128:(tci + 1) * 128],
                                projT[:, hp, :], start=True, stop=True),
                                wait=w, inc=(TT if hpi == 3 else None))
                        last_tt = tdd
                        ei, es, ew = erot.get()
                        tmx = p.emit("vector", lambda e, ps2=ps2, es=es: e.reduce_max(
                            esc[:, es, 0:8],
                            ps2[:, 0:512].rearrange("q (g j) -> q g j", j=64),
                            axis=AX), wait=[tdd] + ew, inc=V)
                        pool.done(pi2, tmx)
                        if do_runmax:
                            tur = p.emit("vector", lambda e, h2=h2, es=es: e.tensor_tensor(
                                runmax[:, h2 * 8:(h2 + 1) * 8],
                                runmax[:, h2 * 8:(h2 + 1) * 8],
                                esc[:, es, 0:8], OP.max), wait=[t_rm], inc=V)
                            erot.done(ei, tur)
                        else:
                            erot.done(ei, tmx)
                return last_tt

            prev_tt = None
            for tb in range(NB):
                prev_tt = kq_pass(wA, 8, tb, diag_k, True, t_wk, prev_tt)

            pi, ps, fw = pool.get()
            t1 = p.emit("tensor", lambda e, ps=ps: e.transpose(
                ps[0:H, 0:128], runmax[:], eye[:]), wait=[(V, V.n)] + fw, inc=TT)
            t2 = p.emit("vector", lambda e, ps=ps: e.reduce_max(
                esc[0:H, 0, 0:1], ps[0:H, 0:128], axis=AX), wait=[t1], inc=V)
            pool.done(pi, t2)
            pi, ps, fw = pool.get()
            t3 = p.emit("tensor", lambda e, ps=ps: e.transpose(
                ps[0:1, 0:H], esc[0:H, 0, 0:1], eye[0:H, 0:H]), wait=[t2] + fw, inc=TT)
            t4 = p.emit("vector", lambda e, ps=ps: e.tensor_copy(
                rows[0:1, 2 * D:2 * D + H], ps[0:1, 0:H]), wait=[t3], inc=V)
            pool.done(pi, t4)
            pi, ps, fw = pool.get()
            t5 = p.emit("tensor", lambda e, ps=ps: e.matmul(
                ps[:, 0:H], onesrow[:], rows[0:1, 2 * D:2 * D + H],
                start=True, stop=True), wait=[t4, t_or] + fw, inc=TT)
            t6 = p.emit("vector", lambda e, ps=ps: e.tensor_copy(
                stabk[:], ps[:, 0:H]), wait=[t5], inc=V)
            pool.done(pi, t6)
            store_tap("diag_k", diag_k[:], [128, TC, H], F32, [(V, V.n)])
            store_tap("stabk", stabk[:], [128, H], F32, [t6])
            p.barrier()
            if STOP_AFTER == "pass1":
                p.finish([("sync", DMA), ("sync", ST), ("gpsimd", GD)])
                p.build(block)
                return nc, tap_aps
            st1a.close()

            st2 = st1.enter_context(ExitStack())
            v_tc = st2.enter_context(nc.sbuf_tensor("v_tc", [128, D], BF16))
            kp = st2.enter_context(nc.sbuf_tensor("kp", [128, D], BF16))
            kv_acc = st2.enter_context(nc.sbuf_tensor("kv_acc", [64, H, 66], F32))
            fm_buf = st2.enter_context(nc.sbuf_tensor("fm_buf", [128, DC, 128], BF16))
            qp_tm, oc_tm = v_tc, kp
            qp_fm = oc_fm = xq_fm = fm_buf
            work = st2.enter_context(nc.sbuf_tensor("work", [128, 3, D], F32))
            sq_sc = st2.enter_context(nc.sbuf_tensor("sq_sc", [128, 2, 512], F32))
            wC = st2.enter_context(nc.sbuf_tensor("wC", [128, DC, D - 256 if "dbg2b" in taps else D], BF16))
            st2a = st2.enter_context(ExitStack())
            pkv = st2a.enter_context(nc.psum_tensor("pkv", [64, 2048], F32))

            t_kvz = p.emit("vector", lambda e: e.memset(kv_acc[:], 0.0))
            p.emit("vector", lambda e: e.memset(kv_bd[:], 0.0))
            t_bdz = p.emit("vector", lambda e: e.memset(ksum_bd[:], 0.0))

            guard_kq = None
            guard_v = None
            guard_kp = None
            t_kvadd = t_kvz
            for tb in range(NB):
                zk = []
                for oc in range(DC):
                    pi, ps, fw = pool.get()
                    tt = None
                    for dc in range(DC):
                        tt = p.emit("tensor", lambda e, oc=oc, dc=dc, ps=ps, tb=tb: e.matmul(
                            ps[:, 0:512], wA[:, dc, oc * 128:(oc + 1) * 128],
                            xn_fm[:, dc, tb * 512:(tb + 1) * 512],
                            start=(dc == 0), stop=(dc == DC - 1)),
                            wait=(fw if dc == 0 else []),
                            inc=(TT if dc == DC - 1 else None))
                    te = p.emit("vector", lambda e, oc=oc, ps=ps: e.tensor_scalar(
                        kq_blk[:, oc, :], ps[:, 0:512],
                        qcol[:, 8 + oc:9 + oc], None, OP.add), wait=[tt, guard_kq])
                    pool.done(pi, te)
                    zk.append(te)
                for tci in range(4):
                    tc = tb * 4 + tci
                    vtoks = []
                    for hf in range(2):
                        pi, ps, fw = pool.get()
                        tt = None
                        for dc in range(DC):
                            tt = p.emit("tensor", lambda e, dc=dc, hf=hf, ps=ps, tc=tc: e.matmul(
                                ps[:, 0:512], xn_fm[:, dc, tc * 128:(tc + 1) * 128],
                                wB[:, dc, hf * 512:(hf + 1) * 512],
                                start=(dc == 0), stop=False),
                                wait=(fw + [t_bvr, guard_v] if dc == 0 else []),
                                inc=None)
                        tt = p.emit("tensor", lambda e, hf=hf, ps=ps: e.matmul(
                            ps[:, 0:512], onesrow[:], rows[0:1, D + hf * 512:D + (hf + 1) * 512],
                            start=False, stop=True), inc=TT)
                        te = p.emit("scalar", lambda e, hf=hf, ps=ps: e.copy(
                            v_tc[:, hf * 512:(hf + 1) * 512], ps[:, 0:512]),
                            wait=[tt], inc=S)
                        pool.done(pi, te)
                        vtoks.append(te)
                    kptoks = []
                    for hf in range(2):
                        pi, ps, fw = pool.get()
                        tdd = None
                        for hpi in range(4):
                            hp = hf * 4 + hpi
                            w = (fw + [t_pr, zk[hp], guard_kp]) if hpi == 0 else [zk[hp]]
                            tdd = p.emit("tensor", lambda e, hp=hp, hpi=hpi, tci=tci, ps=ps: e.matmul(
                                ps[:, hpi * 128:(hpi + 1) * 128],
                                kq_blk[:, hp, tci * 128:(tci + 1) * 128],
                                projT[:, hp, :], start=True, stop=True),
                                wait=w, inc=(TT if hpi == 3 else None))
                        p.emit("vector", lambda e, tc=tc, hf=hf: e.tensor_tensor(
                            smalls[:, 0:8], diag_k[:, tc, hf * 8:(hf + 1) * 8],
                            stabk[:, hf * 8:(hf + 1) * 8], OP.add))
                        p.emit("vector", lambda e, hf=hf, ps=ps: e.tensor_tensor(
                            work[:, 0, 0:512].rearrange("q (g j) -> q g j", j=64),
                            ps[:, 0:512].rearrange("q (g j) -> q g j", j=64),
                            smalls[:, 0:8].unsqueeze(2).broadcast_to([128, 8, 64]),
                            OP.subtract), wait=[tdd, (p.sems["S"], p.sems["S"].n)])
                        pool.done(pi, (p.sems["V"], p.sems["V"].n))
                        p.emit("scalar", lambda e: e.activation(
                            work[:, 1, 0:512], work[:, 0, 0:512], AF.Exp),
                            wait=[(p.sems["V"], p.sems["V"].n)])
                        tk = p.emit("vector", lambda e, hf=hf: e.tensor_scalar(
                            kp[:, hf * 512:(hf + 1) * 512], work[:, 1, 0:512],
                            0.125, 1e-4, OP.mult, OP.add),
                            wait=[(p.sems["S"], p.sems["S"].n), guard_kp])
                        kptoks.append(tk)
                    tkv = None
                    for h in range(H):
                        w = ([t_kvadd] + kptoks + vtoks) if h == 0 else []
                        p.emit("tensor", lambda e, h=h: e.matmul(
                            pkv[0:64, h * 128:h * 128 + 64],
                            kp[:, h * 64:(h + 1) * 64], v_tc[:, h * 64:(h + 1) * 64],
                            start=True, stop=True), wait=w)
                        tkv = p.emit("tensor", lambda e, h=h: e.matmul(
                            pkv[0:64, h * 128 + 64:h * 128 + 65],
                            kp[:, h * 64:(h + 1) * 64], onescol_bf[:, 0:1],
                            start=True, stop=True),
                            inc=(TT if h == H - 1 else None))
                    guard_v = tkv
                    guard_kp = tkv
                    t_kvadd = p.emit("vector", lambda e: e.tensor_tensor(
                        kv_acc[0:64, :, 0:65],
                        kv_acc[0:64, :, 0:65],
                        pkv[0:64, 0:2048].rearrange("q (h c) -> q h c", c=128)[:, :, 0:65],
                        OP.add), wait=[tkv])
                guard_kq = tkv

            store_tap("kv", kv_acc[:], [64, H, 66], F32, [t_kvadd])
            kv_r = kv_acc[:].rearrange("q (hp wh) c -> q hp wh c", wh=2)
            p.emit("gpsimd", lambda e: e.dma_start(
                out=kv_bd[0:64, :, 0:64], in_=kv_r[0:64, :, 0, 0:64],
                allow_slow_non_contiguous=True), wait=[t_kvadd, t_bdz], inc=GD, inc_amt=16)
            p.emit("gpsimd", lambda e: e.dma_start(
                out=kv_bd[64:128, :, 64:128], in_=kv_r[0:64, :, 1, 0:64],
                allow_slow_non_contiguous=True), inc=GD, inc_amt=16)
            p.emit("gpsimd", lambda e: e.dma_start(
                out=ksum_bd[0:64, :, 0:1], in_=kv_r[0:64, :, 0, 64:65],
                allow_slow_non_contiguous=True), inc=GD, inc_amt=16)
            t_kvbf = p.emit("gpsimd", lambda e: e.dma_start(
                out=ksum_bd[64:128, :, 1:2], in_=kv_r[0:64, :, 1, 64:65],
                allow_slow_non_contiguous=True), inc=GD, inc_amt=16)
            p.barrier()
            if STOP_AFTER == "pass2a":
                p.finish([("sync", DMA), ("sync", ST), ("gpsimd", GD)])
                p.build(block)
                return nc, tap_aps
            st2a.close()
            pdg_t = st2.enter_context(nc.psum_tensor("pdiag", [128, 64], F32))
            ptA = st2.enter_context(nc.psum_tensor("ptA", [128, 512], F32))
            ptB = st2.enter_context(nc.psum_tensor("ptB", [128, 512], F32))
            ptr = Rot([ptA, ptB])

            t_wq = load_wT("q", wA, extra_wait=[guard_kq])
            t_wo = load_wT("o", wB, extra_wait=[guard_v])
            if "dbg2b" in taps:
                t_wfq = p.emit("vector", lambda e: e.memset(wC[:], 0.0))
            else:
                t_wfq = load_wT("fq", wC, extra_wait=[(GD, GD.n)])

            dbg = st2.enter_context(nc.sbuf_tensor("dbg", [128, 4, 512], BF16)) if "dbg2b" in taps else None
            if dbg is not None:
                p.emit("vector", lambda e: e.memset(dbg[:], 0.0))
            if dbg is not None:
                p.emit("vector", lambda e: e.tensor_copy(dbg[:, 1, :], wA[:, 0, 0:512]),
                       wait=[t_wq])
            guard_kq = None
            guard_qp = None
            guard_oc = None
            guard_xq = None
            guard_sq = [None, None]
            guard_pd = None
            guard_qpfm = None
            guard_w2 = None
            for tb in range(NB):
                zq = []
                for oc in range(DC):
                    pi, ps, fw = pool.get()
                    tt = None
                    for dc in range(DC):
                        tt = p.emit("tensor", lambda e, oc=oc, dc=dc, ps=ps, tb=tb: e.matmul(
                            ps[:, 0:512], wA[:, dc, oc * 128:(oc + 1) * 128],
                            xn_fm[:, dc, tb * 512:(tb + 1) * 512],
                            start=(dc == 0), stop=(dc == DC - 1)),
                            wait=(fw + [t_wq] if dc == 0 else []),
                            inc=(TT if dc == DC - 1 else None))
                    tz_ = p.emit("vector", lambda e, oc=oc, ps=ps: e.tensor_scalar(
                        work[:, 2, 0:512], ps[:, 0:512],
                        qcol[:, 0 + oc:1 + oc], None, OP.add),
                        wait=[tt, (p.sems["S"], p.sems["S"].n), guard_w2])
                    tb16 = p.emit("scalar", lambda e, oc=oc: e.copy(
                        kq_blk[:, oc, :], work[:, 2, 0:512]), wait=[tz_, guard_kq], inc=S)
                    tsq = p.emit("vector", lambda e, oc=oc: e.tensor_tensor(
                        sq_sc[:, oc % 2, :], work[:, 2, 0:512], work[:, 2, 0:512],
                        OP.mult), wait=[guard_sq[oc % 2]])
                    pool.done(pi, tsq)
                    if dbg is not None and tb == 0 and oc == 0:
                        p.emit("vector", lambda e: e.tensor_copy(dbg[:, 1, :], sq_sc[:, 0, :]),
                               wait=[tsq])
                    zq.append((tb16, tsq))
                    pdg = pdg_t
                    tdg = None
                    for hf in range(2):
                        h = oc * 2 + hf
                        for tci in range(4):
                            w = ([t_oc, tsq, guard_pd] if (oc == 0 and hf == 0 and tci == 0)
                                 else ([tsq] if (hf == 0 and tci == 0) else []))
                            tdg = p.emit("tensor", lambda e, h=h, hf=hf, oc=oc, tci=tci, pdg=pdg: e.matmul(
                                pdg[:, tci * 16 + h:tci * 16 + h + 1],
                                sq_sc[hf * 64:(hf + 1) * 64, oc % 2, tci * 128:(tci + 1) * 128],
                                onescol[hf * 64:(hf + 1) * 64, 0:1],
                                start=True, stop=True), wait=w,
                                inc=(TT if (hf == 1 and tci == 3) else None))
                    guard_sq[oc % 2] = tdg
                if dbg is not None and tb == 0:
                    p.emit("vector", lambda e: e.tensor_copy(dbg[:, 0, 0:64], pdg_t[:, 0:64]),
                           wait=[tdg])
                    store_tap("dbg2b", dbg[:], [128, 4, 512], BF16,
                              [(p.sems["V"], p.sems["V"].n)])
                tdq = None
                for tci in range(4):
                    tdq = p.emit("vector", lambda e, tb=tb, tci=tci, pdg=pdg: e.tensor_scalar(
                        diag_q[:, tb * 4 + tci, :], pdg[:, tci * 16:(tci + 1) * 16],
                        DIAG_C, None, OP.mult), wait=[tdg])
                guard_pd = tdq
                guard_sq = [tdq, tdq]

                for tci in range(4):
                    tc = tb * 4 + tci
                    for hf in range(2):
                        pi, ps, fw = pool.get()
                        tdd = None
                        for hpi in range(4):
                            hp = hf * 4 + hpi
                            w = (fw + [zq[hp][0], guard_qp]) if hpi == 0 else [zq[hp][0]]
                            tdd = p.emit("tensor", lambda e, hp=hp, hpi=hpi, tci=tci, ps=ps: e.matmul(
                                ps[:, hpi * 128:(hpi + 1) * 128],
                                kq_blk[:, hp, tci * 128:(tci + 1) * 128],
                                projT[:, hp, :], start=True, stop=True),
                                wait=w, inc=(TT if hpi == 3 else None))
                        p.emit("vector", lambda e, ps=ps: e.reduce_max(
                            smalls[:, 8:16],
                            ps[:, 0:512].rearrange("q (g j) -> q g j", j=64),
                            axis=AX), wait=[tdd, tdq])
                        p.emit("vector", lambda e, tc=tc, hf=hf: e.tensor_tensor(
                            smalls[:, 0:8], diag_q[:, tc, hf * 8:(hf + 1) * 8],
                            smalls[:, 8:16], OP.add))
                        p.emit("vector", lambda e, hf=hf, ps=ps: e.tensor_tensor(
                            work[:, 0, 0:512].rearrange("q (g j) -> q g j", j=64),
                            ps[:, 0:512].rearrange("q (g j) -> q g j", j=64),
                            smalls[:, 0:8].unsqueeze(2).broadcast_to([128, 8, 64]),
                            OP.subtract), wait=[(p.sems["S"], p.sems["S"].n)])
                        pool.done(pi, (p.sems["V"], p.sems["V"].n))
                        if dbg is not None and tb == 0 and tci == 0 and hf == 0:
                            p.emit("vector", lambda e: e.tensor_copy(dbg[:, 2, :], work[:, 0, 0:512]))
                            p.emit("vector", lambda e: e.tensor_copy(dbg[:, 3, :], kq_blk[:, 0, :]))
                        p.emit("scalar", lambda e: e.activation(
                            work[:, 1, 0:512], work[:, 0, 0:512], AF.Exp),
                            wait=[(p.sems["V"], p.sems["V"].n)])
                        p.emit("vector", lambda e, hf=hf: e.tensor_scalar(
                            qp_tm[:, hf * 512:(hf + 1) * 512], work[:, 1, 0:512],
                            0.125, 1e-4, OP.mult, OP.add),
                            wait=[(p.sems["S"], p.sems["S"].n), guard_qp])
                    t_qp = (p.sems["V"], p.sems["V"].n)
                    if STOP_AFTER == "qp0" and tb == 0 and tci == 0:
                        p.finish([("sync", DMA), ("sync", ST), ("gpsimd", GD)])
                        p.build(block)
                        return nc, tap_aps
                    if dbg is not None and tb == 0 and tci == 0:
                        p.emit("vector", lambda e: e.tensor_copy(dbg[:, 0, :], qp_tm[:, 0:512]))
                        p.emit("vector", lambda e: e.tensor_copy(dbg[:, 1, 0:16], smalls[:, 0:16]))
                    tqf = None
                    for c in range(DC):
                        pi, ps, fw = ptr.get()
                        tt = p.emit("tensor", lambda e, c=c, ps=ps: e.transpose(
                            ps[:].bitcast(BF16)[:, 0:128],
                            qp_tm[:, c * 128:(c + 1) * 128], eyebf[:]),
                            wait=[t_qp, t_eyebf] + fw, inc=TT)
                        tqf = p.emit("scalar", lambda e, c=c, ps=ps: e.copy(
                            qp_fm[:, c, :], ps[:].bitcast(BF16)[:, 0:128]),
                            wait=[tt, guard_qpfm], inc=S)
                        ptr.done(pi, tqf)
                    guard_qp = tqf
                    pn_i, pn, pn_w = pool.get()
                    tnm = None
                    for hp in range(8):
                        w = (pn_w + [tqf, t_kvbf]) if hp == 0 else []
                        tnm = p.emit("tensor", lambda e, hp=hp, pn=pn: e.matmul(
                            pn[:, hp * 2:hp * 2 + 2],
                            qp_fm[:, hp, :], ksum_bd[:, hp, :],
                            start=True, stop=True),
                            wait=w, inc=(TT if hp == 7 else None))
                    po = []
                    for hf in range(2):
                        po_i, po_ps, po_w = pool.get()
                        tqkv = None
                        for hh in range(4):
                            hp = hf * 4 + hh
                            w = po_w if hh == 0 else []
                            tqkv = p.emit("tensor", lambda e, hp=hp, hh=hh, po_ps=po_ps: e.matmul(
                                po_ps[:, hh * 128:(hh + 1) * 128],
                                qp_fm[:, hp, :], kv_bd[:, hp, :],
                                start=True, stop=True),
                                wait=w, inc=(TT if hh == 3 else None))
                        po.append((po_i, po_ps, tqkv))
                    guard_qpfm = po[-1][2]
                    p.emit("vector", lambda e, pn=pn: e.tensor_scalar(
                        smalls[:, 0:16], pn[:, 0:16], 0.01, None, OP.max), wait=[tnm])
                    pool.done(pn_i, (p.sems["V"], p.sems["V"].n))
                    t_rn = emit_recip(p, smalls[:, 0:16], smalls[:, 16:32],
                                      smalls[:, 32:48],
                                      wait=[(p.sems["V"], p.sems["V"].n)])
                    toc = None
                    for hf in range(2):
                        po_i, po_ps, tqkv = po[hf]
                        toc = p.emit("vector", lambda e, hf=hf, po_ps=po_ps: e.tensor_tensor(
                            oc_tm[:, hf * 512:(hf + 1) * 512].rearrange("q (g j) -> q g j", j=64),
                            po_ps[:, 0:512].rearrange("q (g j) -> q g j", j=64),
                            smalls[:, 16 + hf * 8:24 + hf * 8].unsqueeze(2).broadcast_to([128, 8, 64]),
                            OP.mult), wait=[tqkv, t_rn, guard_oc])
                        pool.done(po_i, toc)
                    tof = None
                    for c in range(DC):
                        pi, ps, fw = ptr.get()
                        tt = p.emit("tensor", lambda e, c=c, ps=ps: e.transpose(
                            ps[:].bitcast(BF16)[:, 0:128],
                            oc_tm[:, c * 128:(c + 1) * 128], eyebf[:]),
                            wait=[toc] + fw, inc=TT)
                        tof = p.emit("scalar", lambda e, c=c, ps=ps: e.copy(
                            oc_fm[:, c, :], ps[:].bitcast(BF16)[:, 0:128]),
                            wait=[tt, guard_qpfm], inc=S)
                        ptr.done(pi, tof)
                    guard_oc = tof
                    pa = []
                    for hf in range(2):
                        pi, ps, fw = pool.get()
                        tt = None
                        for dc in range(DC):
                            tt = p.emit("tensor", lambda e, dc=dc, hf=hf, ps=ps: e.matmul(
                                ps[:, 0:512], oc_fm[:, dc, :],
                                wB[:, dc, hf * 512:(hf + 1) * 512],
                                start=(dc == 0), stop=False),
                                wait=(fw + [tof, t_wo, t_bo] if dc == 0 else []))
                        tt = p.emit("tensor", lambda e, hf=hf, ps=ps: e.matmul(
                            ps[:, 0:512], onesrow[:], rows[0:1, hf * 512:(hf + 1) * 512],
                            start=False, stop=True), inc=TT)
                        pa.append((pi, ps, tt))
                    txn = p.emit("vector", lambda e, tc=tc: e.tensor_scalar(
                        work[:, 0, :], acc[:, tc, :],
                        stats[:, tc, 0:1], stats[:, tc, 1:2], OP.subtract, OP.mult),
                        wait=[(p.sems["S"], p.sems["S"].n)])
                    tam = p.emit("vector", lambda e: e.reduce_max(
                        smalls[:, 48:49], work[:, 0, :], axis=AX,
                        apply_absolute_value=True), wait=[txn])
                    p.emit("vector", lambda e: e.tensor_scalar(
                        smalls[:, 48:49], smalls[:, 48:49], 1e-8, None, OP.max))
                    t_sc = emit_recip(p, smalls[:, 48:49], smalls[:, 49:50],
                                      smalls[:, 50:51],
                                      wait=[(p.sems["V"], p.sems["V"].n)])
                    p.emit("vector", lambda e: e.tensor_scalar(
                        smalls[:, 49:50], smalls[:, 49:50], 127.0, None, OP.mult),
                        wait=[t_sc])
                    p.emit("vector", lambda e: e.tensor_scalar(
                        smalls[:, 51:52], smalls[:, 48:49], 1.0 / 127.0, None, OP.mult))
                    p.emit("vector", lambda e: e.tensor_scalar(
                        work[:, 1, :], work[:, 0, :], smalls[:, 49:50], None, OP.mult))
                    for rh in range(2):
                        emit_round(p, V, work[:, 1, rh * 512:(rh + 1) * 512],
                                   work[:, 2, rh * 512:(rh + 1) * 512],
                                   sq_sc[:, 0, :], sq_sc[:, 1, :])
                    tclp = p.emit("vector", lambda e: e.tensor_scalar(
                        work[:, 2, :], work[:, 2, :], -128.0, 127.0, OP.max, OP.min))
                    txf = None
                    for c in range(DC):
                        pi, ps, fw = ptr.get()
                        tt = p.emit("tensor", lambda e, c=c, ps=ps: e.transpose(
                            ps[:, 0:128], work[:, 2, c * 128:(c + 1) * 128], eye[:]),
                            wait=[tclp, t_eye] + fw, inc=TT)
                        txf = p.emit("scalar", lambda e, c=c, ps=ps: e.copy(
                            xq_fm[:, c, :], ps[:, 0:128]),
                            wait=[tt, guard_xq, pa[-1][2]], inc=S)
                        ptr.done(pi, txf)
                    guard_w2 = txf
                    guard_xq = None
                    pf = []
                    for hf in range(2):
                        pi, ps, fw = pool.get()
                        tt = None
                        for dc in range(DC):
                            tt = p.emit("tensor", lambda e, dc=dc, hf=hf, ps=ps: e.matmul(
                                ps[:, 0:512], xq_fm[:, dc, :],
                                wC[:, dc, 0:512] if ("dbg2b" in taps and hf == 1) else wC[:, dc, hf * 512:(hf + 1) * 512],
                                start=(dc == 0), stop=(dc == DC - 1)),
                                wait=(fw + [txf, t_wfq] if dc == 0 else []),
                                inc=(TT if dc == DC - 1 else None))
                        pf.append((pi, ps, tt))
                    guard_xq_l = []
                    for hf in range(2):
                        pai, pas, tat = pa[hf]
                        pfi, pfs, tft = pf[hf]
                        tf0 = p.emit("vector", lambda e, hf=hf, pfs=pfs: e.scalar_tensor_tensor(
                            work[:, 1, hf * 512:(hf + 1) * 512], pfs[:, 0:512],
                            smalls[:, 51:52], wsrep[:, hf * 512:(hf + 1) * 512],
                            OP.mult, OP.mult), wait=[tft])
                        pool.done(pfi, tf0)
                        ta1 = p.emit("vector", lambda e, tc=tc, hf=hf, pas=pas: e.tensor_tensor(
                            acc[:, tc, hf * 512:(hf + 1) * 512],
                            acc[:, tc, hf * 512:(hf + 1) * 512],
                            pas[:, 0:512], OP.add), wait=[tat])
                        pool.done(pai, ta1)
                        p.emit("vector", lambda e, tc=tc, hf=hf: e.tensor_tensor(
                            acc[:, tc, hf * 512:(hf + 1) * 512],
                            acc[:, tc, hf * 512:(hf + 1) * 512],
                            work[:, 1, hf * 512:(hf + 1) * 512], OP.add))
                        guard_xq_l.append(p.emit("vector", lambda e, tc=tc, hf=hf: e.tensor_tensor(
                            acc[:, tc, hf * 512:(hf + 1) * 512],
                            acc[:, tc, hf * 512:(hf + 1) * 512],
                            bfrep[:, hf * 512:(hf + 1) * 512], OP.add)))
                    guard_xq = guard_xq_l[-1]
                    guard_qpfm = pf[-1][2]
                    if STOP_AFTER == "tc0end" and tb == 0 and tci == 0:
                        p.finish([("sync", DMA), ("sync", ST), ("gpsimd", GD)])
                        p.build(block)
                        return nc, tap_aps
                guard_kq = tdd
            store_tap("diagq", diag_q[:], [128, TC, H], F32, [(p.sems["V"], p.sems["V"].n)])
            store_tap("qz_last", kq_blk[:], [128, DC, 512], BF16, [(p.sems["V"], p.sems["V"].n)])
            store_tap("qp_last", qp_tm[:], [128, D], BF16, [(p.sems["V"], p.sems["V"].n)])
            p.barrier()

        with ExitStack() as st3:
            h_fm = xn_fm
            VS, SS = p.sems["V"], p.sems["S"]

            def vtok():
                return (VS, VS.n)

            def stok():
                return (SS, SS.n)

            def ttok():
                return (TT, TT.n)

            n1rep = st3.enter_context(nc.sbuf_tensor("n1rep", [128, D], F32, side="right"))
            n2rep = st3.enter_context(nc.sbuf_tensor("n2rep", [128, D], F32, side="right"))
            ptA2 = st3.enter_context(nc.psum_tensor("ptA2", [128, 512], F32))
            ptB2 = st3.enter_context(nc.psum_tensor("ptB2", [128, 512], F32))
            ptr2 = Rot([ptA2, ptB2])
            stP = st3.enter_context(ExitStack())
            scr = stP.enter_context(nc.sbuf_tensor("scr3", [128, 3, T], F32, side="right"))
            stg = stP.enter_context(nc.sbuf_tensor("stg", [128, DC, 128], BF16, side="right"))
            rows2_l1 = stP.enter_context(nc.sbuf_tensor("rows2_l1", [1, 2 * D + 64], F32, side="right"))

            t_wdec = p.emit("gpsimd", lambda e: e.dma_start(
                out=wA[:], in_=wdec_h.rearrange("(c p) o -> p c o", p=128)), inc=GD, inc_amt=16)
            t_wgat = p.emit("gpsimd", lambda e: e.dma_start(
                out=wB[:], in_=wgat_h.rearrange("(c p) o -> p c o", p=128)), inc=GD, inc_amt=16)
            t_bd = p.emit("sync", lambda e: e.dma_start(
                out=qcol[:, 0:8], in_=bdecay_h.rearrange("(c p) -> p c", p=128),
                allow_slow_non_contiguous=True), inc=DMA, inc_amt=16)
            t_bg = p.emit("sync", lambda e: e.dma_start(
                out=qcol[:, 8:16], in_=bgate_h.rearrange("(c p) -> p c", p=128),
                allow_slow_non_contiguous=True), inc=DMA, inc_amt=16)
            t_n1 = p.emit("sync", lambda e: e.dma_start(
                out=rows2_l1[0:1, 0:D], in_=n1w_h.unsqueeze(0)), inc=DMA, inc_amt=16)
            t_n2 = p.emit("sync", lambda e: e.dma_start(
                out=rows2_l1[0:1, D:2 * D], in_=n2w_h.unsqueeze(0)), inc=DMA, inc_amt=16)
            t_ssl = p.emit("sync", lambda e: e.dma_start(
                out=rows2_l1[0:1, 2 * D:2 * D + 1], in_=ss_h), inc=DMA, inc_amt=16)
            t_msl = p.emit("sync", lambda e: e.dma_start(
                out=rows2_l1[0:1, 2 * D + 1:2 * D + 2], in_=msc_h), inc=DMA, inc_amt=16)
            n1w_rep, n2w_rep = n1rep, n2rep
            store_tap("rows2atmm", rows2_l1[0:1, :], [1, 2 * D + 64], F32, [t_n1, t_or])
            for dst, off, w0 in ((n1w_rep, 0, t_n1), (n2w_rep, D, t_n2)):
                for half in range(2):
                    pi, ps, fw = pool.get()
                    tt = p.emit("tensor", lambda e, off=off, half=half, ps=ps: e.matmul(
                        ps[:, 0:512], onesrow[:],
                        rows2_l1[0:1, off + half * 512:off + (half + 1) * 512],
                        start=True, stop=True), wait=[w0, t_or] + fw, inc=TT)
                    te = p.emit("vector", lambda e, half=half, ps=ps, dst=dst: e.tensor_copy(
                        dst[:, half * 512:(half + 1) * 512], ps[:, 0:512]), wait=[tt])
                    pool.done(pi, te)
            for col, off, w0 in ((52, 2 * D, t_ssl), (53, 2 * D + 1, t_msl)):
                pi, ps, fw = pool.get()
                tt = p.emit("tensor", lambda e, off=off, ps=ps: e.matmul(
                    ps[:, 0:1], onesrow[:], rows2_l1[0:1, off:off + 1],
                    start=True, stop=True), wait=[w0, t_or] + fw, inc=TT)
                te = p.emit("vector", lambda e, col=col, ps=ps: e.tensor_copy(
                    smalls[:, col:col + 1], ps[:, 0:1]), wait=[tt])
                pool.done(pi, te)

            tp = store_tap("n1rep", n1rep[:], [128, D], F32, [(V, V.n)])
            p.emit("vector", None, wait=[tp], inc=V)
            tp = store_tap("rows2l1", rows2_l1[0:1, :], [1, 2 * D + 64], F32, [(V, V.n)])
            p.emit("vector", None, wait=[tp], inc=V)
            t_hlast = None
            t_spill = None
            for tc in range(TC):
                cc = [stats[:, tc, j:j + 1] for j in range(8)]
                p.emit("vector", lambda e, a=cc[2], src=acc[:, tc, :]: e.reduce_sum(
                    a, src, axis=AX))
                tqs = p.emit("scalar", lambda e, src=acc[:, tc, :]: e.activation(
                    scr[:, 2, 0:D], src, AF.Square), wait=[vtok()])
                p.emit("vector", lambda e, a=cc[2]: e.tensor_scalar(
                    a, a, 1.0 / D, None, OP.mult))
                p.emit("vector", lambda e, a=cc[4]: e.reduce_sum(
                    a, scr[:, 2, 0:D], axis=AX), wait=[tqs])
                p.emit("vector", lambda e, a=cc[4]: e.tensor_scalar(
                    a, a, 1.0 / D, None, OP.mult))
                p.emit("vector", lambda e, a=cc[5], m_=cc[2]: e.scalar_tensor_tensor(
                    a, m_, -1.0, m_, OP.mult, OP.mult))
                p.emit("vector", lambda e, a=cc[4], b=cc[5]: e.tensor_tensor(a, a, b, OP.add))
                tvu = p.emit("vector", lambda e, a=cc[4]: e.tensor_scalar(
                    a, a, 1e-5, None, OP.add))
                emit_rsqrt(p, cc[4], cc[3], cc[5], wait=[tvu])
                txn1 = p.emit("vector", lambda e, tc=tc: e.tensor_scalar(
                    scr[:, 0, 0:D], acc[:, tc, :], stats[:, tc, 2:3], stats[:, tc, 3:4],
                    OP.subtract, OP.mult), wait=[stok(), ttok()])
                tq2 = p.emit("scalar", lambda e: e.activation(
                    scr[:, 2, 0:D], scr[:, 0, 0:D], AF.Square), wait=[txn1])
                p.emit("vector", lambda e, a=cc[4]: e.reduce_sum(
                    a, scr[:, 2, 0:D], axis=AX), wait=[tq2])
                p.emit("vector", lambda e, a=cc[4]: e.tensor_scalar(
                    a, a, 1.0 / D, None, OP.mult))
                tvu2 = p.emit("vector", lambda e, a=cc[4]: e.tensor_scalar(
                    a, a, 1e-6, None, OP.add))
                emit_rsqrt(p, cc[4], cc[6], cc[5], wait=[tvu2])
                th = p.emit("vector", lambda e, tc=tc: e.scalar_tensor_tensor(
                    scr[:, 1, 0:D], scr[:, 0, 0:D], stats[:, tc, 6:7], n1w_rep[:],
                    OP.mult, OP.mult), wait=[stok(), ttok()])
                tsg = None
                for dc in range(DC):
                    pi, ps, fw = ptr2.get()
                    tt = p.emit("tensor", lambda e, dc=dc, ps=ps: e.transpose(
                        ps[:, 0:128], scr[:, 0, dc * 128:(dc + 1) * 128], eye[:]),
                        wait=[th, t_eye, t_spill] + fw, inc=TT)
                    tsg = p.emit("scalar", lambda e, dc=dc, ps=ps: e.copy(
                        stg[:, dc, :], ps[:, 0:128]), wait=[tt], inc=S)
                    ptr2.done(pi, tsg)
                t_spill = p.emit("sync", lambda e, tc=tc: e.dma_start(
                    out=xn1_sp[:, :, tc * 128:(tc + 1) * 128], in_=stg[:]),
                    wait=[tsg], inc=DMA, inc_amt=16)
                for dc in range(DC):
                    pi, ps, fw = ptr2.get()
                    tt = p.emit("tensor", lambda e, dc=dc, ps=ps: e.transpose(
                        ps[:, 0:128],
                        scr[:, 1, dc * 128:(dc + 1) * 128], eye[:]),
                        wait=[th] + fw, inc=TT)
                    t_hlast = p.emit("scalar", lambda e, tc=tc, dc=dc, ps=ps: e.copy(
                        h_fm[:, dc, tc * 128:(tc + 1) * 128],
                        ps[:, 0:128]), wait=[tt], inc=S)
                    ptr2.done(pi, t_hlast)
            tp = store_tap("h_fm", h_fm[:], [128, DC, T], BF16, [(S, S.n)])
            p.emit("vector", None, wait=[tp], inc=V)
            tp = store_tap("xn1sp", xn1_sp, [128, DC, T], BF16, [(DMA, DMA.n)])
            p.emit("vector", None, wait=[tp], inc=V)
            p.barrier()
            if STOP_AFTER == "l1p":
                p.finish([("sync", DMA), ("sync", ST), ("gpsimd", GD)])
                p.build(block)
                return nc, tap_aps

            with ExitStack() as stA:
                sst = stA.enter_context(nc.sbuf_tensor("sst", [128, T], BF16, side="right"))
                t_dma_prev = None
                for oc in range(DC):
                    b1, b2, b3 = scr[:, 0, :], scr[:, 1, :], scr[:, 2, :]
                    tse = None
                    for tb in range(NB):
                        pi, ps, fw = pool.get()
                        tt = None
                        for dc in range(DC):
                            tt = p.emit("tensor", lambda e, oc=oc, dc=dc, tb=tb, ps=ps: e.matmul(
                                ps[:, 0:512], wA[:, dc, oc * 128:(oc + 1) * 128],
                                h_fm[:, dc, tb * 512:(tb + 1) * 512],
                                start=(dc == 0), stop=(dc == DC - 1)),
                                wait=(fw + [t_wdec, t_hlast, t_bd] if dc == 0 else []),
                                inc=(TT if dc == DC - 1 else None))
                        tse = p.emit("scalar", lambda e, oc=oc, tb=tb, ps=ps: e.activation(
                            scr[:, 0, tb * 512:(tb + 1) * 512],
                            ps[:, 0:512], AF.Sigmoid, bias=qcol[:, 0 + oc:1 + oc]),
                            wait=[tt, vtok()], inc=S)
                        pool.done(pi, tse)
                    p.emit("vector", lambda e: e.tensor_scalar(
                        b1, b1, 0.05, 0.95, OP.max, OP.min), wait=[tse])
                    tln = p.emit("scalar", lambda e: e.activation(
                        b2, b1, AF.Ln), wait=[vtok()], inc=S)
                    tsc1 = p.emit("vector", lambda e: e.tensor_tensor_scan(
                        b1, b2, b2, 0.0, OP.add, OP.bypass), wait=[tln])
                    tcd = p.emit("scalar", lambda e: e.activation(
                        b3, b1, AF.Exp), wait=[tsc1], inc=S)
                    p.emit("vector", lambda e: e.tensor_tensor(b2, b2, b1, OP.subtract),
                           wait=[tcd])
                    p.emit("vector", lambda e: e.tensor_scalar(
                        b2, b2, 16.11809565095832, None, OP.min))
                    trc = p.emit("scalar", lambda e: e.activation(
                        b1, b2, AF.Exp), wait=[vtok()], inc=S)
                    tge = None
                    for tb in range(NB):
                        pi, ps, fw = pool.get()
                        tt = None
                        for dc in range(DC):
                            tt = p.emit("tensor", lambda e, oc=oc, dc=dc, tb=tb, ps=ps: e.matmul(
                                ps[:, 0:512], wB[:, dc, oc * 128:(oc + 1) * 128],
                                h_fm[:, dc, tb * 512:(tb + 1) * 512],
                                start=(dc == 0), stop=(dc == DC - 1)),
                                wait=(fw + [t_wgat, t_bg, trc] if dc == 0 else []),
                                inc=(TT if dc == DC - 1 else None))
                        tge = p.emit("scalar", lambda e, oc=oc, tb=tb, ps=ps: e.activation(
                            scr[:, 1, tb * 512:(tb + 1) * 512], ps[:, 0:512],
                            AF.Sigmoid, bias=qcol[:, 8 + oc:9 + oc]),
                            wait=[tt, trc], inc=S)
                        pool.done(pi, tge)
                    p.emit("vector", lambda e, oc=oc: e.tensor_tensor(
                        b2, b2, h_fm[:, oc, :], OP.mult), wait=[tge])
                    p.emit("vector", lambda e: e.tensor_tensor(b2, b2, b1, OP.mult))
                    tsc2 = p.emit("vector", lambda e: e.tensor_tensor_scan(
                        b1, b2, b2, 0.0, OP.add, OP.bypass))
                    tst = p.emit("vector", lambda e: e.scalar_tensor_tensor(
                        sst[:], b1, smalls[:, 52:53], b3, OP.mult, OP.mult),
                        wait=[t_dma_prev])
                    t_dma_prev = p.emit("sync", lambda e, oc=oc: e.dma_start(
                        out=state_sp[:, oc, :], in_=sst[:]), wait=[tst],
                        inc=DMA, inc_amt=16)
            tp = store_tap("statesp", state_sp, [128, DC, T], BF16, [(DMA, DMA.n)])
            p.emit("vector", None, wait=[tp], inc=V)
            p.barrier()
            if STOP_AFTER == "l1a":
                p.finish([("sync", DMA), ("sync", ST), ("gpsimd", GD)])
                p.build(block)
                return nc, tap_aps

            with ExitStack() as stB:
                wcv = stB.enter_context(nc.sbuf_tensor("wcv", [128, DC, 7], F32, side="right"))
                lst = stB.enter_context(nc.sbuf_tensor("lst", [128, T], BF16, side="right"))
                t_cw = p.emit("sync", lambda e: e.dma_start(
                    out=wcv[:], in_=convw_h.rearrange("(c p) k -> p c k", p=128),
                    allow_slow_non_contiguous=True), inc=DMA, inc_amt=16)
                t_dma_prev = None
                for oc in range(DC):
                    b1 = scr[:, 0, :]
                    p.emit("vector", lambda e, oc=oc: e.tensor_scalar(
                        b1, h_fm[:, oc, :], wcv[:, oc, 6:7], None, OP.mult),
                        wait=[t_cw, t_dma_prev])
                    for j in range(5, -1, -1):
                        sh = 6 - j
                        p.emit("vector", lambda e, oc=oc, j=j, sh=sh: e.scalar_tensor_tensor(
                            b1[:, sh:T], h_fm[:, oc, 0:T - sh], wcv[:, oc, j:j + 1],
                            b1[:, sh:T], OP.mult, OP.add))
                    tl = p.emit("vector", lambda e: e.tensor_copy(lst[:], b1))
                    t_dma_prev = p.emit("sync", lambda e, oc=oc: e.dma_start(
                        out=local_sp[:, oc, :], in_=lst[:]), wait=[tl],
                        inc=DMA, inc_amt=16)
            tp = store_tap("localsp", local_sp, [128, DC, T], BF16, [(DMA, DMA.n)])
            p.emit("vector", None, wait=[tp], inc=V)
            p.barrier()
            if STOP_AFTER == "l1b":
                p.finish([("sync", DMA), ("sync", ST), ("gpsimd", GD)])
                p.build(block)
                return nc, tap_aps
            stP.close()
            stW.close()

            stDE = st3.enter_context(ExitStack())
            with ExitStack() as stC:
                mst = stC.enter_context(nc.sbuf_tensor("mst", [128, T], BF16, side="right"))
                wk1s = stC.enter_context(nc.sbuf_tensor("wk1s", [128, DC, DM], BF16, side="right"))
                mts = stC.enter_context(nc.sbuf_tensor("mts", [128, 2, DM], BF16, side="right"))
                wms = stC.enter_context(nc.sbuf_tensor("wms", [128, 2, D], BF16, side="right"))
                k1s = stC.enter_context(nc.sbuf_tensor("k1s", [128, 2, T], BF16, side="right"))
                mos = stC.enter_context(nc.sbuf_tensor("mos", [128, 2, T], BF16, side="right"))
                t_wk1 = p.emit("gpsimd", lambda e: e.dma_start(
                    out=wk1s[:], in_=wk1_h.rearrange("(c p) o -> p c o", p=128)),
                    inc=GD, inc_amt=16)
                t_mc = p.emit("gpsimd", lambda e: e.dma_start(
                    out=mbf_h, in_=m_h), inc=GD, inc_amt=16)
                t_wms = p.emit("gpsimd", lambda e: e.dma_start(
                    out=wms[:], in_=wmem_h.rearrange("(c p) o -> p c o", p=128)),
                    inc=GD, inc_amt=16)
                t_mt = None
                for mc in range(2):
                    t_mt = p.emit("sync", lambda e, mc=mc: e.dma_start_transpose(
                        mts[:, mc, :], mbf_h[:, mc * 128:(mc + 1) * 128]),
                        wait=([t_mc] if mc == 0 else []), inc=DMA, inc_amt=16)
                tk1 = None
                for o2 in range(2):
                    for tb in range(NB):
                        pi, ps, fw = pool.get()
                        tt = None
                        for dc in range(DC):
                            tt = p.emit("tensor", lambda e, o2=o2, dc=dc, tb=tb, ps=ps: e.matmul(
                                ps[:, 0:512], wk1s[:, dc, o2 * 128:(o2 + 1) * 128],
                                h_fm[:, dc, tb * 512:(tb + 1) * 512],
                                start=(dc == 0), stop=(dc == DC - 1)),
                                wait=(fw + [t_wk1] if dc == 0 else []),
                                inc=(TT if dc == DC - 1 else None))
                        tk1 = p.emit("scalar", lambda e, o2=o2, tb=tb, ps=ps: e.copy(
                            k1s[:, o2, tb * 512:(tb + 1) * 512], ps[:, 0:512]),
                            wait=[tt], inc=S)
                        pool.done(pi, tk1)
                tmo = None
                for o2 in range(2):
                    for tb in range(NB):
                        pi, ps, fw = pool.get()
                        tt = None
                        for mc in range(2):
                            tt = p.emit("tensor", lambda e, o2=o2, mc=mc, tb=tb, ps=ps: e.matmul(
                                ps[:, 0:512], mts[:, mc, o2 * 128:(o2 + 1) * 128],
                                k1s[:, mc, tb * 512:(tb + 1) * 512],
                                start=(mc == 0), stop=(mc == 1)),
                                wait=(fw + [t_mt, tk1] if mc == 0 else []),
                                inc=(TT if mc == 1 else None))
                        tmo = p.emit("scalar", lambda e, o2=o2, tb=tb, ps=ps: e.copy(
                            mos[:, o2, tb * 512:(tb + 1) * 512], ps[:, 0:512]),
                            wait=[tt], inc=S)
                        pool.done(pi, tmo)
                t_mdma = None
                for oc in range(DC):
                    tme = None
                    for tb in range(NB):
                        pi, ps, fw = pool.get()
                        tt = None
                        for mc in range(2):
                            tt = p.emit("tensor", lambda e, oc=oc, mc=mc, tb=tb, ps=ps: e.matmul(
                                ps[:, 0:512], wms[:, mc, oc * 128:(oc + 1) * 128],
                                mos[:, mc, tb * 512:(tb + 1) * 512],
                                start=(mc == 0), stop=(mc == 1)),
                                wait=(fw + [t_wms, tmo] if mc == 0 else []),
                                inc=(TT if mc == 1 else None))
                        tme = p.emit("vector", lambda e, oc=oc, tb=tb, ps=ps: e.tensor_scalar(
                            mst[:, tb * 512:(tb + 1) * 512], ps[:, 0:512],
                            smalls[:, 53:54], None, OP.mult),
                            wait=[tt, t_mdma])
                        pool.done(pi, tme)
                    t_mdma = p.emit("sync", lambda e, oc=oc: e.dma_start(
                        out=mem_sp[:, oc, :], in_=mst[:]), wait=[tme],
                        inc=DMA, inc_amt=16)
            tp = store_tap("memsp", mem_sp, [128, DC, T], BF16, [(DMA, DMA.n)])
            p.emit("vector", None, wait=[tp], inc=V)
            p.barrier()
            if STOP_AFTER == "l1c":
                p.finish([("sync", DMA), ("sync", ST), ("gpsimd", GD)])
                p.build(block)
                return nc, tap_aps

            if True:
                mixed_bf = stDE.enter_context(nc.sbuf_tensor("mixed_bf", [128, TC, D], BF16, side="right"))
                with ExitStack() as stD:
                    wcs = stD.enter_context(nc.sbuf_tensor("wcs", [128, 24, 512], BF16, side="right"))
                    stb = stD.enter_context(nc.sbuf_tensor("stb", [128, 1, DC, 512], BF16, side="right"))
                    lob = stD.enter_context(nc.sbuf_tensor("lob", [128, 1, DC, 512], BF16, side="right"))
                    meb = stD.enter_context(nc.sbuf_tensor("meb", [128, 1, DC, 512], BF16, side="right"))
                    scrD = stD.enter_context(nc.sbuf_tensor("scrD", [128, 3, 512], F32, side="right"))
                    bcr = rows
                    t_bc = p.emit("sync", lambda e: e.dma_start(
                        out=bcr[0:1, 0:D], in_=bc_h.unsqueeze(0)), inc=DMA, inc_amt=16)
                    t_ms2 = None
                    t_mm_done = [None]
                    for hf in range(2):
                        t_wc = p.emit("gpsimd", lambda e, hf=hf: e.dma_start(
                            out=wcs[:], in_=wc_h[:, hf * 512:(hf + 1) * 512].rearrange(
                                "(c p) o -> p c o", p=128)),
                            wait=[(TT, TT.n)], inc=GD, inc_amt=16)
                        for tb in range(NB):
                            sl = 0
                            t_s1 = p.emit("sync", lambda e, tb=tb, sl=sl: e.dma_start(
                                out=stb[:, sl, :, :],
                                in_=state_sp[:, :, tb * 512:(tb + 1) * 512]),
                                wait=[t_mm_done[sl]], inc=DMA, inc_amt=16)
                            p.emit("sync", lambda e, tb=tb, sl=sl: e.dma_start(
                                out=lob[:, sl, :, :],
                                in_=local_sp[:, :, tb * 512:(tb + 1) * 512]),
                                inc=DMA, inc_amt=16)
                            t_s2 = p.emit("sync", lambda e, tb=tb, sl=sl: e.dma_start(
                                out=meb[:, sl, :, :],
                                in_=mem_sp[:, :, tb * 512:(tb + 1) * 512]),
                                inc=DMA, inc_amt=16)
                            tt = None
                            for tci in range(4):
                                tc = tb * 4 + tci
                                pi, ps, fw = pool.get()
                                first = True
                                for si, src in enumerate(("loc", "st", "mem")):
                                    for dc in range(DC):
                                        if src == "loc":
                                            lt = lob[:, sl, dc, tci * 128:(tci + 1) * 128]
                                        elif src == "st":
                                            lt = stb[:, sl, dc, tci * 128:(tci + 1) * 128]
                                        else:
                                            lt = meb[:, sl, dc, tci * 128:(tci + 1) * 128]
                                        w = (fw + [t_wc, t_s2, t_bc] if first else [])
                                        tt = p.emit("tensor", lambda e, lt=lt, si=si, dc=dc, ps=ps, f=first: e.matmul(
                                            ps[:, 0:512], lt, wcs[:, si * 8 + dc, :],
                                            start=f, stop=False), wait=w)
                                        first = False
                                tt = p.emit("tensor", lambda e, ps=ps, hf=hf: e.matmul(
                                    ps[:, 0:512], onesrow[:],
                                    bcr[0:1, hf * 512:(hf + 1) * 512],
                                    start=False, stop=True), inc=TT)
                                t_mm_done[sl] = tt
                                p.emit("vector", lambda e, tc=tc, hf=hf: e.tensor_scalar(
                                    scrD[:, 0, :], acc[:, tc, hf * 512:(hf + 1) * 512],
                                    stats[:, tc, 2:3], stats[:, tc, 3:4],
                                    OP.subtract, OP.mult), wait=[stok()])
                                p.emit("vector", lambda e, ps=ps: e.tensor_tensor(
                                    scrD[:, 1, :], scrD[:, 0, :], ps[:, 0:512], OP.add),
                                    wait=[tt, stok()])
                                tacc = p.emit("vector", lambda e, tc=tc, hf=hf: e.tensor_tensor(
                                    acc[:, tc, hf * 512:(hf + 1) * 512],
                                    acc[:, tc, hf * 512:(hf + 1) * 512],
                                    scrD[:, 1, :], OP.add))
                                pool.done(pi, tacc)
                                tmxb = p.emit("scalar", lambda e, tc=tc, hf=hf: e.copy(
                                    mixed_bf[:, tc, hf * 512:(hf + 1) * 512],
                                    scrD[:, 1, :]), wait=[vtok()], inc=S)
                                tsq3 = p.emit("scalar", lambda e: e.activation(
                                    scrD[:, 2, :], scrD[:, 1, :], AF.Square), inc=S)
                                t_ms2 = p.emit("vector", lambda e, tc=tc, hf=hf: e.reduce_sum(
                                    stats[:, tc, 6 + hf:7 + hf], scrD[:, 2, :], axis=AX),
                                    wait=[tsq3])
                    for tc in range(TC):
                        cc = [stats[:, tc, j:j + 1] for j in range(8)]
                        p.emit("vector", lambda e, a=cc[4], b=cc[6], c2=cc[7]: e.tensor_tensor(
                            a, b, c2, OP.add), wait=[t_ms2])
                        p.emit("vector", lambda e, a=cc[4]: e.tensor_scalar(
                            a, a, 1.0 / D, None, OP.mult))
                        tvu = p.emit("vector", lambda e, a=cc[4]: e.tensor_scalar(
                            a, a, 1e-6, None, OP.add))
                        emit_rsqrt(p, cc[4], cc[5], cc[2], wait=[tvu])
                tp = store_tap("acc_d", acc[:], [128, TC, D], F32, [(V, V.n)])
                p.emit("vector", None, wait=[tp], inc=V)
                tp = store_tap("mixed", mixed_bf[:], [128, TC, D], BF16, [(V, V.n)])
                p.emit("vector", None, wait=[tp], inc=V)
                tp = store_tap("statsD", stats[:], [128, TC, 8], F32, [(V, V.n)])
                p.emit("vector", None, wait=[tp], inc=V)
                p.barrier()
                if STOP_AFTER == "l1d":
                    p.finish([("sync", DMA), ("sync", ST), ("gpsimd", GD)])
                    p.build(block)
                    return nc, tap_aps

                h2_fm = stDE.enter_context(nc.sbuf_tensor("h2_fm", [128, DC, T], BF16, side="right"))
                with ExitStack() as stE:
                    th2 = None
                    with nc.sbuf_tensor("scrE", [128, D], F32, side="right") as scrE:
                        for tc in range(TC):
                            te = p.emit("vector", lambda e, tc=tc: e.scalar_tensor_tensor(
                                scrE[:], mixed_bf[:, tc, :], stats[:, tc, 5:6], n2w_rep[:],
                                OP.mult, OP.mult), wait=[stok(), ttok()])
                            for dc in range(DC):
                                pi, ps, fw = ptr2.get()
                                tt = p.emit("tensor", lambda e, dc=dc, ps=ps: e.transpose(
                                    ps[:, 0:128],
                                    scrE[:, dc * 128:(dc + 1) * 128], eye[:]),
                                    wait=[te] + fw, inc=TT)
                                th2 = p.emit("scalar", lambda e, tc=tc, dc=dc, ps=ps: e.copy(
                                    h2_fm[:, dc, tc * 128:(tc + 1) * 128],
                                    ps[:, 0:128]), wait=[tt], inc=S)
                                ptr2.done(pi, th2)
                    gsb = stE.enter_context(nc.sbuf_tensor("gsb", [128, 512], BF16, side="right"))
                    prod = stE.enter_context(nc.sbuf_tensor("prod", [128, 4, 512], BF16, side="right"))
                    wgs = stE.enter_context(nc.sbuf_tensor("wgs", [128, DC, 512], BF16, side="right"))
                    wus = stE.enter_context(nc.sbuf_tensor("wus", [128, DC, 512], BF16, side="right"))
                    wds = stE.enter_context(nc.sbuf_tensor("wds", [128, 4, D], BF16, side="right"))
                    t_wg = t_wu = t_wd = None
                    t_down_done = None
                    for fg in range(8):
                        t_wg = p.emit("gpsimd", lambda e, fg=fg: e.dma_start(
                            out=wgs[:], in_=wg_h[:, fg * 512:(fg + 1) * 512].rearrange(
                                "(c p) f -> p c f", p=128)),
                            wait=[(TT, TT.n)], inc=GD, inc_amt=16)
                        t_wu = p.emit("gpsimd", lambda e, fg=fg: e.dma_start(
                            out=wus[:], in_=wup_h[:, fg * 512:(fg + 1) * 512].rearrange(
                                "(c p) f -> p c f", p=128)), inc=GD, inc_amt=16)
                        t_wd = p.emit("gpsimd", lambda e, fg=fg: e.dma_start(
                            out=wds[:], in_=wdn_h[fg * 512:(fg + 1) * 512, :].rearrange(
                                "(c p) o -> p c o", p=128)), inc=GD, inc_amt=16)
                        for tb in range(NB):
                            tpr = None
                            for fc in range(4):
                                pi, ps, fw = pool.get()
                                tt = None
                                for dc in range(DC):
                                    tt = p.emit("tensor", lambda e, fc=fc, dc=dc, tb=tb, ps=ps: e.matmul(
                                        ps[:, 0:512], wgs[:, dc, fc * 128:(fc + 1) * 128],
                                        h2_fm[:, dc, tb * 512:(tb + 1) * 512],
                                        start=(dc == 0), stop=(dc == DC - 1)),
                                        wait=(fw + [t_wg, th2, t_down_done] if dc == 0 else []),
                                        inc=(TT if dc == DC - 1 else None))
                                tg = p.emit("scalar", lambda e, ps=ps: e.activation(
                                    gsb[:], ps[:, 0:512], AF.Silu), wait=[tt, vtok()], inc=S)
                                pool.done(pi, tg)
                                pi2, ps2, fw2 = pool.get()
                                tt2 = None
                                for dc in range(DC):
                                    tt2 = p.emit("tensor", lambda e, fc=fc, dc=dc, tb=tb, ps2=ps2: e.matmul(
                                        ps2[:, 0:512], wus[:, dc, fc * 128:(fc + 1) * 128],
                                        h2_fm[:, dc, tb * 512:(tb + 1) * 512],
                                        start=(dc == 0), stop=(dc == DC - 1)),
                                        wait=(fw2 + [t_wu] if dc == 0 else []),
                                        inc=(TT if dc == DC - 1 else None))
                                tpr = p.emit("vector", lambda e, fc=fc, ps2=ps2: e.tensor_tensor(
                                    prod[:, fc, :],
                                    gsb[:], ps2[:, 0:512], OP.mult),
                                    wait=[tt2, tg, t_down_done])
                                pool.done(pi2, tpr)
                            t_prod_done = tpr
                            tdd_ = None
                            for ti in range(4):
                                tc = tb * 4 + ti
                                for hf in range(2):
                                    pi, ps, fw = pool.get()
                                    tt = None
                                    for fc in range(4):
                                        tt = p.emit("tensor", lambda e, fc=fc, ti=ti, hf=hf, ps=ps: e.matmul(
                                            ps[:, 0:512], prod[:, fc, ti * 128:(ti + 1) * 128],
                                            wds[:, fc, hf * 512:(hf + 1) * 512],
                                            start=(fc == 0), stop=(fc == 3)),
                                            wait=(fw + [t_wd, t_prod_done] if fc == 0 else []),
                                            inc=(TT if fc == 3 else None))
                                    ta = p.emit("vector", lambda e, tc=tc, hf=hf, ps=ps: e.tensor_tensor(
                                        acc[:, tc, hf * 512:(hf + 1) * 512],
                                        acc[:, tc, hf * 512:(hf + 1) * 512],
                                        ps[:, 0:512], OP.add), wait=[tt])
                                    pool.done(pi, ta)
                                    tdd_ = ta
                            t_down_done = tdd_
                tp = store_tap("acc_e", acc[:], [128, TC, D], F32, [(V, V.n)])
                p.emit("vector", None, wait=[tp], inc=V)
                p.barrier()
                if STOP_AFTER == "l1e":
                    p.finish([("sync", DMA), ("sync", ST), ("gpsimd", GD)])
                    p.build(block)
                    return nc, tap_aps

            stDE.close()
            with ExitStack() as stF:
                xn1f = xn_fm
                ctm = stF.enter_context(nc.sbuf_tensor("ctm", [128, TC, E], F32, side="right"))
                cfm = stF.enter_context(nc.sbuf_tensor("cfm", [16, T], BF16, side="right"))
                b1c = stF.enter_context(nc.sbuf_tensor("b1c", [128, E, 16], F32, side="right"))
                b2s = stF.enter_context(nc.sbuf_tensor("b2s", [16, D], BF16, side="right"))
                stR = stF.enter_context(ExitStack())
                scrF = stR.enter_context(nc.sbuf_tensor("scrF", [128, 2 * E], F32, side="right"))
                lgs = stR.enter_context(nc.sbuf_tensor("lgs", [16, T], F32, side="right"))
                lgt = stR.enter_context(nc.sbuf_tensor("lgt", [128, E], F32, side="right"))
                rws = stR.enter_context(nc.sbuf_tensor("rws", [128, DC, E], BF16, side="right"))
                rbc = stR.enter_context(nc.sbuf_tensor("rbc", [16, 1], F32, side="right"))
                t_xn1 = p.emit("sync", lambda e: e.dma_start(
                    out=xn1f[:], in_=xn1_sp), wait=[(TT, TT.n)], inc=DMA, inc_amt=16)
                t_rw = p.emit("gpsimd", lambda e: e.dma_start(
                    out=rws[:], in_=rw_h.rearrange("(c p) o -> p c o", p=128)),
                    inc=GD, inc_amt=16)
                t_rb = p.emit("sync", lambda e: e.dma_start(
                    out=rbc[:], in_=rb_h.unsqueeze(1)), inc=DMA, inc_amt=16)
                t_b1 = p.emit("sync", lambda e: e.dma_start(
                    out=b1c[:], in_=b1_h.rearrange("e (c p) -> p e c", p=128),
                    allow_slow_non_contiguous=True), inc=DMA, inc_amt=16)
                t_b2 = p.emit("gpsimd", lambda e: e.dma_start(
                    out=b2s[:], in_=b2_h), inc=GD, inc_amt=16)
                tlg = None
                for tb in range(NB):
                    pi, ps, fw = pool.get()
                    tt = None
                    for dc in range(DC):
                        tt = p.emit("tensor", lambda e, dc=dc, tb=tb, ps=ps: e.matmul(
                            ps[0:16, 0:512], rws[:, dc, :],
                            xn1f[:, dc, tb * 512:(tb + 1) * 512],
                            start=(dc == 0), stop=(dc == DC - 1)),
                            wait=(fw + [t_rw, t_xn1, t_rb] if dc == 0 else []),
                            inc=(TT if dc == DC - 1 else None))
                    te = p.emit("vector", lambda e, tb=tb, ps=ps: e.tensor_scalar(
                        lgs[0:16, tb * 512:(tb + 1) * 512], ps[0:16, 0:512],
                        rbc[0:16, 0:1], None, OP.add), wait=[tt])
                    pool.done(pi, te)
                for tc in range(TC):
                    pi, ps, fw = ptr2.get()
                    tt = p.emit("tensor", lambda e, tc=tc, ps=ps: e.transpose(
                        ps[0:128, 0:16], lgs[0:16, tc * 128:(tc + 1) * 128],
                        eye[0:16, 0:16]), wait=[vtok()] + fw, inc=TT)
                    tlt = p.emit("vector", lambda e, ps=ps: e.tensor_copy(
                        lgt[:], ps[0:128, 0:16]), wait=[tt, stok()])
                    ptr2.done(pi, tlt)
                    p.emit("vector", lambda e: e.reduce_max(
                        smalls[:, 54:55], lgt[:], axis=AX))
                    p.emit("vector", lambda e: e.tensor_copy(smalls[:, 56:57], smalls[:, 54:55]))
                    p.emit("vector", lambda e, tc=tc: e.tensor_copy(ctm[:, tc, :], lgt[:]))
                    for r_ in range(3):
                        p.emit("vector", lambda e, tc=tc: e.tensor_scalar(
                            scrF[:, 0:E], ctm[:, tc, :], smalls[:, 54:55], None, OP.is_ge))
                        p.emit("vector", lambda e, tc=tc: e.scalar_tensor_tensor(
                            ctm[:, tc, :], scrF[:, 0:E], -1e30, ctm[:, tc, :],
                            OP.mult, OP.add))
                        p.emit("vector", lambda e, tc=tc: e.reduce_max(
                            smalls[:, 54:55], ctm[:, tc, :], axis=AX))
                    p.emit("vector", lambda e: e.tensor_scalar(
                        scrF[:, 0:E], lgt[:], smalls[:, 54:55], None, OP.is_ge))
                    p.emit("vector", lambda e: e.tensor_scalar(
                        smalls[:, 57:58], smalls[:, 56:57], -1.0, None, OP.mult),
                        wait=[stok()])
                    tex = p.emit("scalar", lambda e: e.activation(
                        scrF[:, E:2 * E], lgt[:], AF.Exp, bias=smalls[:, 57:58]),
                        wait=[vtok()], inc=S)
                    p.emit("vector", lambda e: e.tensor_tensor(
                        scrF[:, 0:E], scrF[:, 0:E], scrF[:, E:2 * E], OP.mult),
                        wait=[tex])
                    p.emit("vector", lambda e: e.reduce_sum(
                        smalls[:, 58:59], scrF[:, 0:E], axis=AX))
                    trs = emit_recip(p, smalls[:, 58:59], smalls[:, 59:60],
                                     smalls[:, 60:61], wait=[vtok()])
                    tc_ = p.emit("vector", lambda e, tc=tc: e.tensor_scalar(
                        ctm[:, tc, :], scrF[:, 0:E], smalls[:, 59:60], None, OP.mult),
                        wait=[trs])
                    pi, ps, fw = ptr2.get()
                    tt = p.emit("tensor", lambda e, tc=tc, ps=ps: e.transpose(
                        ps[0:16, 0:128], ctm[:, tc, :], eye[:]),
                        wait=[tc_] + fw, inc=TT)
                    tcf = p.emit("vector", lambda e, tc=tc, ps=ps: e.tensor_copy(
                        cfm[0:16, tc * 128:(tc + 1) * 128], ps[0:16, 0:128]), wait=[tt])
                    ptr2.done(pi, tcf)
                for tc in range(TC):
                    for hf in range(2):
                        pi, ps, fw = pool.get()
                        tt = p.emit("tensor", lambda e, tc=tc, hf=hf, ps=ps: e.matmul(
                            ps[:, 0:512], cfm[0:16, tc * 128:(tc + 1) * 128],
                            b2s[0:16, hf * 512:(hf + 1) * 512],
                            start=True, stop=True), wait=fw + [t_b2, (VS, VS.n)], inc=TT)
                        ta = p.emit("vector", lambda e, tc=tc, hf=hf, ps=ps: e.tensor_tensor(
                            acc[:, tc, hf * 512:(hf + 1) * 512],
                            acc[:, tc, hf * 512:(hf + 1) * 512],
                            ps[:, 0:512], OP.add), wait=[tt])
                        pool.done(pi, ta)
                tp = store_tap("ctm", ctm[:], [128, TC, E], F32, [(V, V.n)])
                p.emit("vector", None, wait=[tp], inc=V)
                tp = store_tap("lgsF", lgs[:], [16, T], F32, [(V, V.n)])
                p.emit("vector", None, wait=[tp], inc=V)
                p.barrier()
                if STOP_AFTER == "l1r":
                    p.finish([("sync", DMA), ("sync", ST), ("gpsimd", GD)])
                    p.build(block)
                    return nc, tap_aps
                stR.close()
                w1s = stF.enter_context(nc.sbuf_tensor("w1s", [128, DC, I], BF16, side="right"))
                w2s = stF.enter_context(nc.sbuf_tensor("w2s", [128, 16, D], BF16, side="right"))
                hmid = stF.enter_context(nc.sbuf_tensor("hmid", [128, 16, 512], BF16, side="right"))
                t_mm1_done = None
                t_mm2_done = None
                for e_ in range(E):
                    t_w1 = p.emit("gpsimd", lambda e, e_=e_: e.dma_start(
                        out=w1s[:], in_=w1_h[e_].rearrange("(c p) i -> p c i", p=128)),
                        wait=([(TT, TT.n)] if e_ == 0 else [t_mm1_done]), inc=GD, inc_amt=16)
                    t_w2 = p.emit("gpsimd", lambda e, e_=e_: e.dma_start(
                        out=w2s[:], in_=w2_h[e_].rearrange("(c p) o -> p c o", p=128)),
                        wait=[t_mm2_done], inc=GD, inc_amt=16)
                    for tb in range(NB):
                        tgl = None
                        for ic in range(16):
                            pi, ps, fw = pool.get()
                            tt = None
                            for dc in range(DC):
                                tt = p.emit("tensor", lambda e, ic=ic, dc=dc, tb=tb, ps=ps: e.matmul(
                                    ps[:, 0:512], w1s[:, dc, ic * 128:(ic + 1) * 128],
                                    xn1f[:, dc, tb * 512:(tb + 1) * 512],
                                    start=(dc == 0), stop=(dc == DC - 1)),
                                    wait=(fw + [t_w1, t_mm2_done] if dc == 0 else []),
                                    inc=(TT if dc == DC - 1 else None))
                            tgl = p.emit("scalar", lambda e, e_=e_, ic=ic, ps=ps: e.activation(
                                hmid[:, ic, :], ps[:, 0:512], AF.Gelu,
                                bias=b1c[:, e_, ic:ic + 1]), wait=[tt, t_b1, t_mm2_done], inc=S)
                            pool.done(pi, tgl)
                        t_mm1_done = tgl
                        for tci in range(4):
                            tc = tb * 4 + tci
                            for hf in range(2):
                                pi, ps, fw = pool.get()
                                tt = None
                                for ic in range(16):
                                    tt = p.emit("tensor", lambda e, ic=ic, tci=tci, hf=hf, ps=ps: e.matmul(
                                        ps[:, 0:512], hmid[:, ic, tci * 128:(tci + 1) * 128],
                                        w2s[:, ic, hf * 512:(hf + 1) * 512],
                                        start=(ic == 0), stop=(ic == 15)),
                                        wait=(fw + [t_w2, tgl] if ic == 0 else []),
                                        inc=(TT if ic == 15 else None))
                                ta = p.emit("vector", lambda e, e_=e_, tc=tc, hf=hf, ps=ps: e.scalar_tensor_tensor(
                                    acc[:, tc, hf * 512:(hf + 1) * 512], ps[:, 0:512],
                                    ctm[:, tc, e_:e_ + 1],
                                    acc[:, tc, hf * 512:(hf + 1) * 512],
                                    OP.mult, OP.add), wait=[tt])
                                pool.done(pi, ta)
                                t_mm2_done = ta
            p.barrier()

        for tc in range(TC):
            p.emit("sync", lambda e, tc=tc: e.dma_start(
                out=out_r[:, tc, :], in_=acc[:, tc, :]), inc=ST, inc_amt=16)
        p.finish([("sync", DMA), ("sync", ST), ("gpsimd", GD)])
        p.build(block)
    return nc, tap_aps


def _host_inputs(inputs, core):
    x = np.ascontiguousarray(np.asarray(inputs["L0_attn_query"])[core])
    return {
        "x": x,
        "wq": np.asarray(inputs["L0_attn_Wq"]),
        "wk": np.asarray(inputs["L0_attn_Wk"]),
        "wv": np.asarray(inputs["L0_attn_Wv"]),
        "wo": np.asarray(inputs["L0_attn_Wo"]),
        "bq": np.asarray(inputs["L0_attn_bq"]),
        "bk": np.asarray(inputs["L0_attn_bk"]),
        "bv": np.asarray(inputs["L0_attn_bv"]),
        "bo": np.asarray(inputs["L0_attn_bo"]),
        "proj": np.asarray(inputs["L0_attn_proj"]),
        "wf": np.asarray(inputs["L0_ffn_weight"]),
        "bf": np.asarray(inputs["L0_ffn_bias"]),
        "wdec": np.asarray(inputs["L1_attn_Wdecay"]),
        "wgat": np.asarray(inputs["L1_attn_Wgate"]),
        "bdec": np.asarray(inputs["L1_attn_bdecay"]),
        "bgat": np.asarray(inputs["L1_attn_bgate"]),
        "convw": np.asarray(inputs["L1_attn_conv_w"]),
        "n1w": np.asarray(inputs["L1_attn_norm1_w"]),
        "n2w": np.asarray(inputs["L1_attn_norm2_w"]),
        "wk1": np.asarray(inputs["L1_attn_Wk"]),
        "mm": np.asarray(inputs["L1_attn_M"]),
        "wmem": np.asarray(inputs["L1_attn_Wmem"]),
        "wc": np.asarray(inputs["L1_attn_Wc"]),
        "bc": np.asarray(inputs["L1_attn_bc"]),
        "wup": np.asarray(inputs["L1_attn_Wup"]),
        "wg": np.asarray(inputs["L1_attn_Wg"]),
        "wdn": np.asarray(inputs["L1_attn_Wdown"]),
        "rw": np.asarray(inputs["L1_ffn_router_w"]),
        "rb": np.asarray(inputs["L1_ffn_router_b"]),
        "w1": np.asarray(inputs["L1_ffn_w1"]),
        "b1": np.asarray(inputs["L1_ffn_b1"]),
        "w2": np.asarray(inputs["L1_ffn_w2"]),
        "b2": np.asarray(inputs["L1_ffn_b2"]),
        "ss": np.asarray(inputs["L1_attn_state_scale"]).reshape(1, 1),
        "msc": np.asarray(inputs["L1_attn_mem_scale"]).reshape(1, 1),
        "eye": np.eye(128, dtype=np.float32),
        "eyebf": np.eye(128, dtype=np.float32).astype(ml_dtypes.bfloat16),
        "onesrow": np.ones((1, 128), dtype=np.float32),
        "onescol": np.ones((128, 1), dtype=np.float32),
    }


def kernel(**inputs):
    nc, _ = build_nc(taps=DEBUG_TAPS)
    in_maps = [_host_inputs(inputs, c) for c in range(8)]
    res = run_bass_kernel_spmd(nc, in_maps, core_ids=list(range(8)))
    return np.stack([np.asarray(r["out"]) for r in res.results], axis=0)
